# version 01
# baseline (speedup 1.0000x reference)
import numpy as np
import concourse.bass as bass
import concourse.mybir as mybir
from concourse.bass_utils import run_bass_kernel_spmd
from concourse.bass_core_rust import IndirectOffsetOnAxis

F32 = mybir.dt.float32
BF16 = mybir.dt.bfloat16
I32 = mybir.dt.int32
AF = mybir.ActivationFunctionType
ALU = mybir.AluOpType
AX = mybir.AxisListType

T = 4096
D = 1024
ENGINES = ("sync", "tensor", "vector", "scalar", "gpsimd")


class SemC:
    __slots__ = ("h", "n", "name")

    def __init__(self, h, name):
        self.h = h
        self.n = 0
        self.name = name


class Plan:
    CHAINED = ("vector", "scalar", "gpsimd")

    def __init__(self, nc):
        self.nc = nc
        self.streams = {e: [] for e in ENGINES}
        self._ctxs = []
        self._wmax = {}
        self._chain = {}

    def sem(self, name):
        cm = self.nc.semaphore(name)
        h = cm.__enter__()
        self._ctxs.append(cm)
        return SemC(h, name)

    def sbuf(self, name, shape, dtype):
        cm = self.nc.sbuf_tensor(name, shape, dtype)
        t = cm.__enter__()
        self._ctxs.append(cm)
        return t

    def psum(self, name, shape, dtype=F32):
        cm = self.nc.psum_tensor(name, shape, dtype)
        t = cm.__enter__()
        self._ctxs.append(cm)
        return t

    def const(self, arr, name):
        return self.nc.inline_tensor(np.ascontiguousarray(arr), name).ap()

    def wait(self, eng, sem, val=None):
        v = sem.n if val is None else val
        key = (eng, id(sem))
        if v > 0 and v > self._wmax.get(key, 0):
            self._wmax[key] = v
            self.streams[eng].append(lambda e, _s=sem.h, _v=v: e.wait_ge(_s, _v))

    def op(self, eng, fn, incs=(), _dma=False):
        chained = eng in self.CHAINED and not _dma
        for s, a in incs:
            s.n += a
        if chained:
            assert len(incs) <= 1
            prev = self._chain.get(eng)
            if incs:
                s, a = incs[0]
                upd = (s.h, 1)
                self._chain[eng] = (s.h, s.n)
            else:
                if ("chn", eng) not in self._chain:
                    self._chain[("chn", eng)] = self.sem(f"chn_{eng}")
                cs = self._chain[("chn", eng)]
                cs.n += 1
                upd = (cs.h, 1)
                self._chain[eng] = (cs.h, cs.n)

            def t(e, _fn=fn, _prev=prev, _upd=upd):
                if _prev is not None:
                    e.wait_ge(_prev[0], _prev[1])
                _fn(e).then_inc(_upd[0], _upd[1])
            self.streams[eng].append(t)
            return
        _incs = tuple((s.h, a) for s, a in incs)

        def t(e, _fn=fn, _incs=_incs):
            ins = _fn(e)
            for h, a in _incs:
                ins = ins.then_inc(h, a)
        self.streams[eng].append(t)

    def dma(self, eng, out, in_, sem=None, **kw):
        incs = ((sem, 16),) if sem is not None else ()
        self.op(eng, lambda e: e.dma_start(out=out, in_=in_, **kw), incs, _dma=True)

    def build(self, final_sems):
        nc = self.nc
        blk = nc.Block()
        block = blk.__enter__()
        fin = [(s.h, s.n) for s in final_sems]

        def mk(ename):
            ths = self.streams[ename]

            def run(e):
                for t in ths:
                    t(e)
                for h, v in fin:
                    e.wait_ge(h, v)
            return run

        block.sync(mk("sync"))
        block.tensor(mk("tensor"))
        block.vector(mk("vector"))
        block.scalar(mk("scalar"))
        block.gpsimd(mk("gpsimd"))
        blk.__exit__(None, None, None)
        for cm in reversed(self._ctxs):
            cm.__exit__(None, None, None)
        return nc


def _wcols(wT, oc=128):
    din, dout = wT.shape
    r = wT.reshape(din // 128, 128, dout // oc, oc).transpose(2, 0, 1, 3)
    return np.ascontiguousarray(r)


def _pp(v):
    n = v.shape[0] // 128
    return np.ascontiguousarray(v.reshape(n, 128).T)


def prep_l1ffn(inp):
    f = lambda a: np.ascontiguousarray(np.asarray(a, dtype=np.float32))
    return {
        "w1r": _wcols(f(inp["L1_ffn_w1_w"]).T),
        "w2r": _wcols(f(inp["L1_ffn_w2_w"]).T),
        "h1r": _wcols(f(inp["L1_ffn_h1_w"]).T),
        "h2r": _wcols(f(inp["L1_ffn_h2_w"]).T),
        "w3r": _wcols(f(inp["L1_ffn_w3_w"]).T),
        "s1": _pp(f(inp["L1_ffn_w1_s"])),
        "s2": _pp(f(inp["L1_ffn_w2_s"])),
        "sh1": _pp(f(inp["L1_ffn_h1_s"])),
        "sh2": _pp(f(inp["L1_ffn_h2_s"])),
        "s3": _pp(f(inp["L1_ffn_w3_s"])),
    }


class LNRes:
    def __init__(self, P, tag, stat_ps=None, bc_ps=None, bc_sb=None):
        self.P = P
        self.u = P.sbuf(f"{tag}_u", [128, 8 * 512], F32)
        self.sq = P.sbuf(f"{tag}_sq", [128, 2 * 512], F32)
        self.rows = P.sbuf(f"{tag}_rows", [1, 3 * 512], F32)
        self.ones_col = P.sbuf(f"{tag}_onec", [128, 1], F32)
        self.ones_row = P.sbuf(f"{tag}_oner", [1, 128], F32)
        self.stat_ps = stat_ps if stat_ps is not None else P.psum(f"{tag}_sps", [1, 1024], F32)
        self.bc_ps = bc_ps if bc_ps is not None else P.psum(f"{tag}_bps", [128, 1024], F32)
        self.bc_sb = bc_sb
        self.xn = P.sbuf(f"{tag}_xn", [128, 2 * 512], F32)
        self.eps = P.sbuf(f"{tag}_eps", [1, 1], F32)
        self.s_mm = P.sem(f"{tag}_mm")
        self.s_a = P.sem(f"{tag}_a")
        self.s_v = P.sem(f"{tag}_v")
        self.s_st = [P.sem(f"{tag}_st0"), P.sem(f"{tag}_st1")]
        self.s_stg = [P.sem(f"{tag}_stg0"), P.sem(f"{tag}_stg1")]
        P.op("vector", lambda e: e.memset(self.ones_col[:, :], 1.0))
        P.op("vector", lambda e: e.memset(self.eps[:, :], 1e-5))
        P.op("vector", lambda e: e.memset(self.ones_row[:, :], 1.0), [(self.s_v, 1)])
        P.wait("tensor", self.s_v)


def emit_ln_tile(P, R, tj, xf_new, xt_new, eps=1e-5, n=1024.0):
    rows = R.rows
    mu = rows[0:1, 0:512]
    m2 = rows[0:1, 512:1024]
    w = rows[0:1, 1024:1536]

    for c in range(8):
        u_c = R.u[:, c * 512:(c + 1) * 512]
        P.op("tensor", lambda e, u_c=u_c, c=c: e.matmul(
            R.stat_ps[0:1, 0:512], R.ones_col[:, :], u_c,
            start=(c == 0), stop=(c == 7)))
    mmb = R.s_mm.n
    for c in range(8):
        u_c = R.u[:, c * 512:(c + 1) * 512]
        sqc = R.sq[:, (c % 2) * 512:(c % 2) * 512 + 512]
        if c >= 2:
            P.wait("scalar", R.s_mm, mmb + c - 1)
        P.op("scalar", lambda e, u_c=u_c, sqc=sqc: e.activation(sqc, u_c, AF.Square),
             [(R.s_a, 1)])
        P.wait("tensor", R.s_a)
        P.op("tensor", lambda e, c=c, sqc=sqc: e.matmul(
            R.stat_ps[0:1, 512:1024], R.ones_col[:, :], sqc,
            start=(c == 0), stop=(c == 7)), [(R.s_mm, 1)])
    P.wait("vector", R.s_mm)
    P.op("vector", lambda e: e.tensor_scalar(mu, R.stat_ps[0:1, 0:512], 1.0 / n, None, ALU.mult))
    P.op("vector", lambda e: e.tensor_scalar(m2, R.stat_ps[0:1, 512:1024], 1.0 / n, None, ALU.mult))
    P.op("vector", lambda e: e.tensor_mul(w, mu, mu))
    P.op("vector", lambda e: e.tensor_sub(w, m2, w), [(R.s_v, 1)])
    P.wait("scalar", R.s_v)
    P.op("scalar", lambda e: e.activation(w, w, AF.Sqrt, bias=R.eps[0:1, 0:1]),
         [(R.s_a, 1)])
    P.wait("vector", R.s_a)
    P.op("vector", lambda e: e.reciprocal(w, w))
    P.op("vector", lambda e: e.tensor_mul(mu, mu, w), [(R.s_v, 1)])
    P.wait("tensor", R.s_v)
    if R.bc_sb is None:
        P.op("tensor", lambda e: e.matmul(R.bc_ps[:, 0:512], R.ones_row[:, :], w,
                                          start=True, stop=True))
        P.op("tensor", lambda e: e.matmul(R.bc_ps[:, 512:1024], R.ones_row[:, :], mu,
                                          start=True, stop=True), [(R.s_mm, 1)])
        P.wait("vector", R.s_mm)
        brstd = R.bc_ps[:, 0:512]
        bmur = R.bc_ps[:, 512:1024]
    else:
        P.op("tensor", lambda e: e.matmul(R.bc_ps[:, 0:512], R.ones_row[:, :], w,
                                          start=True, stop=True), [(R.s_mm, 1)])
        P.wait("vector", R.s_mm)
        P.op("vector", lambda e: e.tensor_copy(R.bc_sb[:, 0:512], R.bc_ps[:, 0:512]),
             [(R.s_v, 1)])
        P.wait("tensor", R.s_v)
        P.op("tensor", lambda e: e.matmul(R.bc_ps[:, 0:512], R.ones_row[:, :], mu,
                                          start=True, stop=True), [(R.s_mm, 1)])
        P.wait("vector", R.s_mm)
        brstd = R.bc_sb[:, 0:512]
        bmur = R.bc_ps[:, 0:512]
    for c in range(8):
        u_c = R.u[:, c * 512:(c + 1) * 512]
        xc = R.xn[:, (c % 2) * 512:(c % 2) * 512 + 512]
        P.wait("vector", R.s_st[c % 2])
        P.wait("vector", R.s_stg[c % 2])
        P.op("vector", lambda e, u_c=u_c, xc=xc, b=brstd: e.tensor_mul(xc, u_c, b))
        P.op("vector", lambda e, xc=xc, b=bmur: e.tensor_sub(xc, xc, b),
             [(R.s_v, 1)])
        P.wait("sync", R.s_v)
        P.dma("sync", xf_new[c * 128:(c + 1) * 128, tj * 512:(tj + 1) * 512],
              xc, sem=R.s_st[c % 2])
        if xt_new is not None:
            P.wait("gpsimd", R.s_v)
            P.dma("gpsimd", xt_new[c * 128:(c + 1) * 128, tj * 512:(tj + 1) * 512],
                  xc, sem=R.s_stg[c % 2])


def emit_l1ffn(P, io):
    s_ldw = [P.sem("ff_ld0"), P.sem("ff_ld1")]
    s_ldw3 = P.sem("ff_ldw3")
    s_lds = P.sem("ff_lds")
    s_mm = P.sem("ff_mm")
    s_act = P.sem("ff_act")
    s_v = P.sem("ff_v")
    s_st = [P.sem("ff_st0"), P.sem("ff_st1")]

    abuf = P.sbuf("ff_ab", [128, 65536], BF16)
    wbuf = P.sbuf("ff_wb", [128, 8192], BF16)
    t1 = P.sbuf("ff_t1", [128, 512], BF16)
    t2 = P.sbuf("ff_t2", [128, 512], BF16)
    htile = P.sbuf("ff_h", [128, 2 * 512], BF16)
    xres = P.sbuf("ff_xr", [128, 8 * 512], F32)
    ps = P.psum("ff_ps", [128, 2048], F32)
    R = LNRes(P, "ffln")

    evac_done = {}
    st_done = {}
    mm_done = {}

    xt = lambda c: abuf[:, c * 4096:(c + 1) * 4096]
    for c in range(8):
        P.dma("sync", xt(c), io["x3t"][c * 128:(c + 1) * 128, :], sem=s_lds)
    lds0 = s_lds.n

    gidx = 0
    for i in range(32):
        sl = i % 2
        if i >= 2:
            P.wait("gpsimd", s_mm, mm_done[("A", i - 2)])
        P.dma("gpsimd", wbuf[:, sl * 2048:sl * 2048 + 1024].rearrange(
                  "p (c x) -> p c x", c=8),
              io["w1r"][i].rearrange("c p x -> p c x"), sem=s_ldw[sl])
        P.dma("gpsimd", wbuf[:, sl * 2048 + 1024:sl * 2048 + 2048].rearrange(
                  "p (c x) -> p c x", c=8),
              io["w2r"][i].rearrange("c p x -> p c x"), sem=s_ldw[sl])
        for j in range(8):
            pj = gidx % 2
            psa = ps[:, pj * 1024:pj * 1024 + 512]
            psb = ps[:, pj * 1024 + 512:pj * 1024 + 1024]
            if j == 0:
                P.wait("tensor", s_ldw[sl])
                if i == 0:
                    P.wait("tensor", s_lds, lds0)
            if gidx >= 2:
                P.wait("tensor", s_act, evac_done[("A", gidx - 2)])
            for c in range(8):
                rhs = abuf[:, c * 4096 + j * 512: c * 4096 + (j + 1) * 512]
                lw = wbuf[:, sl * 2048 + c * 128: sl * 2048 + (c + 1) * 128]
                P.op("tensor", lambda e, o=psa, l=lw, r=rhs, c=c: e.matmul(
                    o, l, r, start=(c == 0), stop=(c == 7)))
            for c in range(8):
                rhs = abuf[:, c * 4096 + j * 512: c * 4096 + (j + 1) * 512]
                lw = wbuf[:, sl * 2048 + 1024 + c * 128: sl * 2048 + 1024 + (c + 1) * 128]
                P.op("tensor", lambda e, o=psb, l=lw, r=rhs, c=c: e.matmul(
                    o, l, r, start=(c == 0), stop=(c == 7)),
                    [(s_mm, 1)] if c == 7 else ())
            mm_done[("A", i)] = s_mm.n
            P.wait("scalar", s_mm)
            if gidx >= 1:
                P.wait("scalar", s_v, gidx)
            P.op("scalar", lambda e, o=psa, i=i: e.activation(
                t1[:, :], o, AF.Silu, scale=io["s1"][:, i:i + 1]))
            P.op("scalar", lambda e, o=psb, i=i: e.activation(
                t2[:, :], o, AF.Copy, scale=io["s2"][:, i:i + 1]), [(s_act, 1)])
            evac_done[("A", gidx)] = s_act.n
            P.wait("vector", s_act)
            hb = htile[:, pj * 512:(pj + 1) * 512]
            P.wait("vector", s_st[pj])
            P.op("vector", lambda e, hb=hb: e.tensor_mul(hb, t1[:, :], t2[:, :]),
                 [(s_v, 1)])
            P.wait("sync", s_v)
            P.dma("sync", io["ha"][i * 128:(i + 1) * 128, j * 512:(j + 1) * 512],
                  hb, sem=s_st[pj])
            gidx += 1

    for (ph, src, dst, wr, sc) in (("B", io["ha"], io["hb"], io["h1r"], io["sh1"]),
                                   ("C", io["hb"], io["ha"], io["h2r"], io["sh2"])):
        P.wait("gpsimd", s_mm)
        P.wait("tensor", s_act)
        for half in range(2):
            P.wait("sync", s_st[0])
            P.wait("sync", s_st[1])
            P.wait("sync", s_mm)
            for c in range(32):
                P.dma("sync", abuf[:, c * 2048:(c + 1) * 2048],
                      src[c * 128:(c + 1) * 128, half * 2048:(half + 1) * 2048],
                      sem=s_lds)
            ldsv = s_lds.n
            gidx = 0
            for i in range(32):
                sl = i % 2
                if half == 1 and i < 2:
                    P.wait("gpsimd", s_mm, mm_done[(ph, 0, 30 + i)])
                elif i >= 2:
                    P.wait("gpsimd", s_mm, mm_done[(ph, half, i - 2)])
                for q in range(4):
                    P.dma("gpsimd", wbuf[:, sl * 4096 + q * 1024:sl * 4096 + (q + 1) * 1024].rearrange(
                              "p (c x) -> p c x", c=8),
                          wr[i, q * 8:(q + 1) * 8].rearrange("c p x -> p c x"),
                          sem=s_ldw[sl])
                for j in range(4):
                    pj = gidx % 4
                    psa = ps[:, pj * 512:(pj + 1) * 512]
                    if j == 0:
                        P.wait("tensor", s_ldw[sl])
                        if i == 0:
                            P.wait("tensor", s_lds, ldsv)
                    if gidx >= 4:
                        P.wait("tensor", s_act, evac_done[(ph, half, gidx - 4)])
                    for c in range(32):
                        rhs = abuf[:, c * 2048 + j * 512: c * 2048 + (j + 1) * 512]
                        lw = wbuf[:, sl * 4096 + c * 128: sl * 4096 + (c + 1) * 128]
                        P.op("tensor", lambda e, o=psa, l=lw, r=rhs, c=c: e.matmul(
                            o, l, r, start=(c == 0), stop=(c == 31)),
                            [(s_mm, 1)] if c == 31 else ())
                    mm_done[(ph, half, i)] = s_mm.n
                    P.wait("scalar", s_mm)
                    hb = htile[:, (gidx % 2) * 512:(gidx % 2) * 512 + 512]
                    P.wait("scalar", s_st[gidx % 2])
                    P.op("scalar", lambda e, o=psa, hb=hb, i=i: e.activation(
                        hb, o, AF.Gelu, scale=sc[:, i:i + 1]), [(s_act, 1)])
                    evac_done[(ph, half, gidx)] = s_act.n
                    P.wait("sync", s_act)
                    P.dma("sync", dst[i * 128:(i + 1) * 128,
                                      half * 2048 + j * 512:half * 2048 + (j + 1) * 512],
                          hb, sem=s_st[gidx % 2])
                    gidx += 1

    P.wait("sync", s_st[0])
    P.wait("sync", s_st[1])
    P.wait("sync", s_mm)
    P.wait("gpsimd", s_mm)
    P.wait("tensor", s_act)
    for i in range(8):
        P.dma("gpsimd", abuf[:, i * 4096:(i + 1) * 4096].rearrange(
                  "p (c x) -> p c x", c=32),
              io["w3r"][i].rearrange("c p x -> p c x"), sem=s_ldw3)
    w3v = s_ldw3.n
    hc = lambda c: abuf[:, 32768 + c * 512: 32768 + (c + 1) * 512]
    d_mm = {}
    for tj in range(8):
        if tj > 0:
            P.wait("sync", s_mm, d_mm[tj - 1])
            P.wait("sync", s_v)
        for c in range(32):
            P.dma("sync", hc(c), io["ha"][c * 128:(c + 1) * 128,
                                          tj * 512:(tj + 1) * 512], sem=s_lds)
        for c in range(8):
            P.dma("sync", xres[:, c * 512:(c + 1) * 512],
                  io["x3f"][c * 128:(c + 1) * 128, tj * 512:(tj + 1) * 512],
                  sem=s_lds)
        ldsv = s_lds.n
        for i in range(8):
            pj = i % 4
            psa = ps[:, pj * 512:(pj + 1) * 512]
            if i == 0:
                P.wait("tensor", s_ldw3, w3v)
                P.wait("tensor", s_lds, ldsv)
            if tj > 0 or i >= 4:
                P.wait("tensor", s_act, evac_done[("D", tj * 8 + i - 4)])
            for c in range(32):
                lw = abuf[:, (i * 32 + c) * 128:(i * 32 + c + 1) * 128]
                P.op("tensor", lambda e, o=psa, l=lw, r=hc(c), c=c: e.matmul(
                    o, l, r, start=(c == 0), stop=(c == 31)),
                    [(s_mm, 1)] if c == 31 else ())
            d_mm[tj] = s_mm.n
            P.wait("scalar", s_mm)
            u_c = R.u[:, i * 512:(i + 1) * 512]
            P.op("scalar", lambda e, o=psa, u_c=u_c, i=i: e.activation(
                u_c, o, AF.Copy, scale=io["s3"][:, i:i + 1]), [(s_act, 1)])
            evac_done[("D", tj * 8 + i)] = s_act.n
            P.wait("vector", s_act)
            P.op("vector", lambda e, u_c=u_c, c=i: e.tensor_add(
                u_c, u_c, xres[:, c * 512:(c + 1) * 512]), [(s_v, 1)])
        P.wait("tensor", s_v)
        P.wait("scalar", s_v)
        emit_ln_tile(P, R, tj, io["outf"], io.get("outt"))
        P.wait("scalar", R.s_v)
        P.wait("scalar", R.s_mm)

    return [s_st[0], s_st[1], R.s_st[0], R.s_st[1], R.s_stg[0], R.s_stg[1]]


def build_l1ffn_standalone():
    nc = bass.Bass("TRN2", target_bir_lowering=False)
    P = Plan(nc)
    io = {}
    io["x3f"] = nc.dram_tensor("x3f", [D, T], F32, kind="ExternalInput").ap()
    io["x3t"] = nc.dram_tensor("x3t", [D, T], BF16, kind="ExternalInput").ap()
    for k, shp in (("w1r", (32, 8, 128, 128)), ("w2r", (32, 8, 128, 128)),
                   ("h1r", (32, 32, 128, 128)), ("h2r", (32, 32, 128, 128)),
                   ("w3r", (8, 32, 128, 128))):
        io[k] = nc.dram_tensor(k, list(shp), F32, kind="ExternalInput").ap()
    for k in ("s1", "s2", "sh1", "sh2"):
        io[k] = nc.dram_tensor(k, [128, 32], F32, kind="ExternalInput").ap()
    io["s3"] = nc.dram_tensor("s3", [128, 8], F32, kind="ExternalInput").ap()
    io["ha"] = nc.dram_tensor("ha", [4096, T], BF16, kind="Internal").ap()
    io["hb"] = nc.dram_tensor("hb", [4096, T], BF16, kind="Internal").ap()
    io["outf"] = nc.dram_tensor("outf", [D, T], F32, kind="ExternalOutput").ap()

    s_pre = P.sem("pre")
    sc_sb = P.sbuf("sc", [128, 4 * 32 + 8], F32)
    P.dma("sync", sc_sb[:, 0:32], io["s1"], sem=s_pre)
    P.dma("sync", sc_sb[:, 32:64], io["s2"], sem=s_pre)
    P.dma("sync", sc_sb[:, 64:96], io["sh1"], sem=s_pre)
    P.dma("sync", sc_sb[:, 96:128], io["sh2"], sem=s_pre)
    P.dma("sync", sc_sb[:, 128:136], io["s3"], sem=s_pre)
    P.wait("scalar", s_pre)
    io2 = dict(io)
    io2["s1"] = sc_sb[:, 0:32]
    io2["s2"] = sc_sb[:, 32:64]
    io2["sh1"] = sc_sb[:, 64:96]
    io2["sh2"] = sc_sb[:, 96:128]
    io2["s3"] = sc_sb[:, 128:136]
    fin = emit_l1ffn(P, io2)
    P.build(fin)
    return nc





ALIBI_SLOPES = [2.0 ** (-0.5 * (h + 1)) for h in range(16)]
ALIBI_W = [min(T, int(60.0 / s) + 1) for s in ALIBI_SLOPES]


def prep_l0attn(inp):
    f = lambda a: np.ascontiguousarray(np.asarray(a, dtype=np.float32))
    w_in = f(inp["L0_attn_w_in"])
    b_in = f(inp["L0_attn_b_in"])
    wv = w_in[:, 2048:3072]
    return {
        "wqkr": _wcols(w_in[:, 0:2048]),
        "wvr": np.ascontiguousarray(
            wv.reshape(8, 128, 2, 512).transpose(2, 0, 1, 3)),
        "woutr": _wcols(f(inp["L0_attn_w_out"])),
        "b_qk": np.ascontiguousarray(b_in[0:2048].reshape(32, 64).T),
        "b_v": np.ascontiguousarray(b_in[2048:3072].reshape(1, 1024)),
        "b_out": _pp(f(inp["L0_attn_b_out"])),
    }


def emit_l0attn(P, io):
    nc = P.nc
    s_ld = P.sem("at_ld")
    s_lds = P.sem("at_lds")
    s_mm = P.sem("at_mm")
    s_act = P.sem("at_act")
    s_v = P.sem("at_v")
    s_st = P.sem("at_st")
    stq = [P.sem("at_stq0"), P.sem("at_stq1")]
    stv = [P.sem("at_stv0"), P.sem("at_stv1")]
    s_ldv = P.sem("at_ldv")
    s_ldo = P.sem("at_ldo")

    c_iotk = P.const(
        (np.arange(32)[None, :] * 128 + np.arange(128)[:, None]).astype(np.float32),
        "c_iotk")
    c_io512 = P.const(np.arange(512, dtype=np.float32)[None, :], "c_io512")
    c_mdiag = P.const(
        np.where(np.arange(128)[:, None] <= np.arange(128)[None, :], 0.0,
                 -1e30).astype(np.float32), "c_mdiag")

    abuf = P.sbuf("at_ab", [128, 49152], BF16)
    fbuf = P.sbuf("at_fb", [128, 5632], F32)
    rows = P.sbuf("at_rows", [1, 4 * 512], F32)
    rows2 = P.sbuf("at_r2", [128, 120], F32)
    mask_sb = P.sbuf("at_msk", [128, 128], F32)
    mk_sb = P.sbuf("at_mk", [1, 2], F32)
    bo = P.sbuf("at_bo", [128, 8], F32)
    psA = P.psum("at_psA", [128, 2048], F32)
    psB = P.psum("at_psB", [128, 2048], F32)
    R = LNRes(P, "atln", stat_ps=psB[0:1, 0:1024], bc_ps=psB[:, 1024:2048])

    P.dma("sync", rows2[:, 0:32], c_iotk, sem=s_lds)
    P.dma("sync", rows[0:1, 512:1024], c_io512, sem=s_lds)
    P.dma("sync", mask_sb[:, :], c_mdiag, sem=s_lds)
    P.dma("sync", bo[:, :], io["b_out"], sem=s_lds)
    P.dma("sync", rows2[0:64, 80:112], io["b_qk"], sem=s_lds)
    P.dma("sync", rows[0:1, 2 * 512:4 * 512], io["b_v"], sem=s_lds)
    pre_lds = s_lds.n
    P.wait("vector", s_lds, pre_lds)
    for qk in range(2):
        for hp in range(8):
            dst = rows2[:, 64 + qk * 8 + hp:65 + qk * 8 + hp]
            P.op("vector", lambda e, d=dst, qk=qk, hp=hp: e.tensor_copy(
                d[0:64, :], rows2[0:64, 80 + qk * 16 + 2 * hp:81 + qk * 16 + 2 * hp]))
            P.op("vector", lambda e, d=dst, qk=qk, hp=hp: e.tensor_copy(
                d[64:128, :], rows2[0:64, 81 + qk * 16 + 2 * hp:82 + qk * 16 + 2 * hp]),
                [(s_v, 1)])
    P.wait("tensor", s_v)
    vb_bc = fbuf[:, 0:1024]
    P.op("tensor", lambda e: e.matmul(psB[:, 1024:1536], R.ones_row[:, :],
                                      rows[0:1, 2 * 512:3 * 512], start=True,
                                      stop=True))
    P.op("tensor", lambda e: e.matmul(psB[:, 1536:2048], R.ones_row[:, :],
                                      rows[0:1, 3 * 512:4 * 512], start=True,
                                      stop=True), [(s_mm, 1)])
    P.wait("vector", s_mm)
    P.op("vector", lambda e: e.tensor_copy(vb_bc, psB[:, 1024:2048]), [(s_v, 1)])

    for i in range(16):
        P.dma("gpsimd", abuf[:, i * 1024:(i + 1) * 1024].rearrange(
                  "p (c x) -> p c x", c=8),
              io["wqkr"][i].rearrange("c p x -> p c x"), sem=s_ld)
    for db in range(2):
        P.dma("gpsimd", abuf[:, 16384 + db * 4096:16384 + (db + 1) * 4096].rearrange(
                  "p (c x) -> p c x", c=8),
              io["wvr"][db].rearrange("c p x -> p c x"), sem=s_ld)
    w_all = s_ld.n
    XS = 16384 + 8192
    xsb = lambda c: abuf[:, XS + c * 512: XS + (c + 1) * 512]
    qkt = P.sbuf("at_qkt", [128, 1024], BF16)
    vtl = P.sbuf("at_vtl", [128, 1024], BF16)
    a1mm = {}
    a1ev = {}
    a1st = {}
    g = 0
    for j in range(8):
        if j >= 1:
            P.wait("gpsimd", s_mm, a1mm[j - 1])
        for c in range(8):
            P.dma("gpsimd", xsb(c),
                  io["xf"][c * 128:(c + 1) * 128, j * 512:(j + 1) * 512], sem=s_ld)
        ldv = s_ld.n
        P.wait("tensor", s_ld, ldv)
        if j == 0:
            P.wait("tensor", s_ld, w_all)
            P.wait("tensor", s_v)
        for hp in range(8):
            for qk in range(2):
                iw = qk * 8 + hp
                pv = psA[:, (g % 2) * 512:(g % 2) * 512 + 512]
                if g >= 2:
                    P.wait("tensor", s_act, a1ev[g - 2])
                for c in range(8):
                    lw = abuf[:, iw * 1024 + c * 128: iw * 1024 + (c + 1) * 128]
                    P.op("tensor", lambda e, o=pv, l=lw, r=xsb(c), c=c: e.matmul(
                        o, l, r, start=(c == 0), stop=(c == 7)),
                        [(s_mm, 1)] if c == 7 else ())
                P.wait("vector", s_mm)
                P.wait("vector", stq[g % 2])
                qslot = qkt[:, (g % 2) * 512:(g % 2) * 512 + 512]
                if qk == 0:
                    P.op("vector", lambda e, o=pv, qs=qslot, hp=hp: e.tensor_scalar(
                        qs, o, rows2[:, 64 + hp:65 + hp], 0.125,
                        ALU.add, ALU.mult), [(s_act, 1)])
                else:
                    P.op("vector", lambda e, o=pv, qs=qslot, hp=hp: e.tensor_scalar(
                        qs, o, rows2[:, 72 + hp:73 + hp], None,
                        ALU.add), [(s_act, 1)])
                a1ev[g] = s_act.n
                P.wait("sync", s_act)
                for half in range(2):
                    h = hp * 2 + half
                    P.dma("sync", io["qk_d"][h, qk, :, j * 512:(j + 1) * 512],
                          qslot[half * 64:(half + 1) * 64, :], sem=stq[g % 2])
                g += 1
        a1mm[j] = s_mm.n
        for tb in range(4):
            for db in range(2):
                gg = tb * 2 + db
                pv = psA[:, 1024 + (gg % 2) * 512:1024 + (gg % 2) * 512 + 512]
                if (j, tb, db) > (0, 0, 1):
                    P.wait("tensor", s_v, a1ev.get(("v", j, gg - 2),
                                                   a1ev.get(("v", j - 1, 6 + gg), 0)))
                for c in range(8):
                    lw = xsb(c)[:, tb * 128:(tb + 1) * 128]
                    rw = abuf[:, 16384 + db * 4096 + c * 512:16384 + db * 4096 + (c + 1) * 512]
                    P.op("tensor", lambda e, o=pv, l=lw, r=rw, c=c: e.matmul(
                        o, l, r, start=(c == 0), stop=(c == 7)),
                        [(s_mm, 1)] if c == 7 else ())
                P.wait("vector", s_mm)
                vslot = vtl[:, db * 512:(db + 1) * 512]
                P.wait("vector", stv[gg % 2])
                P.op("vector", lambda e, o=pv, vs=vslot, db=db: e.tensor_add(
                    vs, o, vb_bc[:, db * 512:(db + 1) * 512]), [(s_v, 1)])
                a1ev[("v", j, gg)] = s_v.n
                P.wait("sync", s_v)
                P.dma("sync", io["v_d"][(j * 4 + tb) * 128:(j * 4 + tb + 1) * 128,
                                        db * 512:(db + 1) * 512],
                      vslot, sem=stv[gg % 2])
        a1mm[j] = s_mm.n

    QX, KX, VX = 0, 4096, 8192
    OB = VX + 32 * 65
    SQ = 1024
    a2mm = {}
    for s in (stq[0], stq[1], stv[0], stv[1]):
        P.wait("sync", s)
    P.wait("sync", s_v)
    P.wait("gpsimd", s_mm)
    nvx = 32
    for h in range(16):
        slope = ALIBI_SLOPES[h]
        qx = abuf[:, QX:QX + 4096]
        kx = abuf[:, KX:KX + 4096]
        if h > 0:
            P.wait("sync", s_mm, a2mm[h - 1])
        P.dma("sync", qx[0:64, :], io["qk_d"][h, 0], sem=s_lds)
        P.dma("sync", kx[0:64, :], io["qk_d"][h, 1], sem=s_lds)
        ldv = s_lds.n
        P.wait("vector", s_lds, ldv)
        P.op("vector", lambda e, kx=kx: e.memset(kx[64:65, :], 1.0))
        P.wait("scalar", s_lds, ldv)
        sqq = fbuf[0:64, 1024:1024 + 512]
        P.op("vector", lambda e: e.memset(mk_sb[0:1, 0:1], 0.0), [(s_v, 1)])
        kq_mm = {}
        for s in range(8):
            P.wait("scalar", s_mm)
            P.op("scalar", lambda e, kx=kx, s=s: e.activation(
                sqq, kx[0:64, s * 512:(s + 1) * 512], AF.Square), [(s_act, 1)])
            P.wait("tensor", s_act)
            P.wait("tensor", s_v)
            P.op("tensor", lambda e: e.matmul(
                psB[0:1, 0:512], R.ones_col[0:64, :], sqq,
                start=True, stop=True), [(s_mm, 1)])
            P.wait("vector", s_mm)
            P.op("vector", lambda e: e.reduce_max(mk_sb[0:1, 1:2], psB[0:1, 0:512],
                                                  axis=AX.X), [(s_v, 1)])
            P.wait("vector", s_v)
            P.op("vector", lambda e: e.tensor_max(mk_sb[0:1, 0:1], mk_sb[0:1, 0:1],
                                                  mk_sb[0:1, 1:2]), [(s_v, 1)])
        P.wait("scalar", s_v)
        P.op("scalar", lambda e: e.activation(mk_sb[0:1, 1:2], mk_sb[0:1, 0:1],
                                              AF.Sqrt), [(s_act, 1)])
        for s in range(8):
            P.wait("scalar", s_mm)
            P.op("scalar", lambda e, qx=qx, s=s: e.activation(
                sqq, qx[0:64, s * 512:(s + 1) * 512], AF.Square), [(s_act, 1)])
            P.wait("tensor", s_act)
            P.op("tensor", lambda e: e.matmul(
                psB[0:1, 0:512], R.ones_col[0:64, :], sqq,
                start=True, stop=True), [(s_mm, 1)])
            P.wait("scalar", s_mm)
            P.wait("scalar", s_v)
            P.op("scalar", lambda e: e.activation(rows[0:1, 0:512], psB[0:1, 0:512],
                                                  AF.Sqrt), [(s_act, 1)])
            P.wait("vector", s_act)
            P.op("vector", lambda e: e.tensor_scalar(
                rows[0:1, 0:512], rows[0:1, 0:512], mk_sb[0:1, 1:2], None, ALU.mult))
            P.op("vector", lambda e, slope=slope, s=s: e.scalar_tensor_tensor(
                rows[0:1, 0:512], rows[0:1, 512:1024], slope,
                rows[0:1, 0:512], ALU.mult, ALU.add))
            P.op("vector", lambda e, slope=slope, s=s: e.tensor_scalar(
                rows[0:1, 0:512], rows[0:1, 0:512], 1.0, slope * s * 512.0,
                ALU.mult, ALU.add))
            P.op("vector", lambda e, qx=qx, s=s: e.tensor_scalar(
                qx[64:65, s * 512:(s + 1) * 512], rows[0:1, 0:512], -1.0, None,
                ALU.mult), [(s_v, 1)])
        P.op("vector", lambda e, slope=slope: e.tensor_scalar(
            rows2[:, 32:64], rows2[:, 0:32], slope, None, ALU.mult), [(s_v, 1)])
        vxa = abuf[:, VX:VX + nvx * 65]
        P.wait("vector", s_st)
        P.wait("vector", s_mm)
        P.op("vector", lambda e, vxa=vxa: e.memset(vxa, 1.0), [(s_v, 1)])
        P.wait("tensor", s_v)
        P.wait("sync", s_v)
        P.wait("scalar", s_v)
        for jq in range(8):
            tq0 = jq * 512
            kstart = max(0, (tq0 - ALIBI_W[h]) // 128)
            if jq > 0:
                P.wait("sync", s_mm, a2pv)
                P.wait("vector", s_st)
            nint = jq * 4 - kstart
            nkt = nint + 4
            P.dma("sync", vxa.rearrange("p (n y) -> p n y", y=65)[:, 0:nkt, 0:64],
                  io["v_d"][kstart * 128:(jq + 1) * 512, h * 64:(h + 1) * 64]
                  .rearrange("(n p) x -> p n x", p=128), sem=s_ldv)
            P.wait("tensor", s_ldv)
            ops = lambda s: psB[:, s * 512:s * 512 + 65]
            pexp = P.a2_pexp
            pv_mm = {}
            for r in range(nint):
                ki = kstart + r
                sl = r % 2
                spv = psA[:, sl * 512:(sl + 1) * 512]
                if r >= 2:
                    P.wait("tensor", s_act)
                P.op("tensor", lambda e, o=spv, kx=kx, qx=qx, ki=ki, tq0=tq0: e.matmul(
                    o, kx[0:65, ki * 128:(ki + 1) * 128],
                    qx[0:65, tq0:tq0 + 512], start=True, stop=True), [(s_mm, 1)])
                P.wait("scalar", s_mm)
                if r >= 4:
                    P.wait("scalar", s_mm, pv_mm[r - 4])
                pb = pexp[:, (r % 4) * 512:(r % 4) * 512 + 512]
                P.op("scalar", lambda e, o=spv, pb=pb, ki=ki: e.activation(
                    pb, o, AF.Exp, bias=rows2[:, 32 + ki:33 + ki]), [(s_act, 1)])
                P.wait("tensor", s_act)
                for sub in range(4):
                    st0 = (r == 0)
                    P.op("tensor", lambda e, pb=pb, sub=sub, rel=ki - kstart, st0=st0:
                         e.matmul(ops(sub),
                                  pb[:, sub * 128:(sub + 1) * 128],
                                  vxa[:, rel * 65:rel * 65 + 65],
                                  start=st0, stop=False),
                         [(s_mm, 1)] if sub == 3 else ())
                pv_mm[r] = s_mm.n
            for d in range(4):
                ki = jq * 4 + d
                w = (4 - d) * 128
                spv = psA[:, 1024:1024 + w]
                if d > 0:
                    P.wait("tensor", s_act)
                P.op("tensor", lambda e, o=spv, kx=kx, qx=qx, ki=ki, tq0=tq0, d=d, w=w:
                     e.matmul(o, kx[0:65, ki * 128:(ki + 1) * 128],
                              qx[0:65, tq0 + d * 128:tq0 + 512],
                              start=True, stop=True), [(s_mm, 1)])
                P.wait("vector", s_mm)
                P.op("vector", lambda e, o=psA[:, 1024:1024 + 128]: e.tensor_add(
                    o, o, mask_sb[:, :]), [(s_v, 1)])
                P.wait("scalar", s_v)
                if d > 0:
                    P.wait("scalar", s_mm, dpv)
                pb = pexp[:, 2048:2048 + w]
                P.op("scalar", lambda e, o=spv, pb=pb, ki=ki: e.activation(
                    pb, o, AF.Exp, bias=rows2[:, 32 + ki:33 + ki]), [(s_act, 1)])
                P.wait("tensor", s_act)
                for s in range(d, 4):
                    st0 = (nint == 0 and d == 0)
                    P.op("tensor", lambda e, pb=pb, s=s, d=d,
                         rel=jq * 4 + d - kstart, st0=st0:
                         e.matmul(ops(s), pb[:, (s - d) * 128:(s - d + 1) * 128],
                                  vxa[:, rel * 65:rel * 65 + 65],
                                  start=st0, stop=(s == d)), [(s_mm, 1)])
                dpv = s_mm.n
            a2pv = s_mm.n
            P.wait("vector", s_mm)
            ob = abuf[:, OB:OB + 256]
            rc = rows2[:, 112:113]
            for sub in range(4):
                P.op("vector", lambda e, sub=sub: e.reciprocal(
                    rc, ops(sub)[:, 64:65]))
                P.op("vector", lambda e, sub=sub: e.tensor_scalar(
                    ob[:, sub * 64:(sub + 1) * 64], ops(sub)[:, 0:64],
                    rc, None, ALU.mult), [(s_v, 1)])
            P.wait("sync", s_v)
            P.dma("sync", io["o_d"][tq0:tq0 + 512, h * 64:(h + 1) * 64]
                  .rearrange("(s p) x -> p s x", p=128),
                  ob.rearrange("p (s x) -> p s x", x=64), sem=s_st)
        a2mm[h] = s_mm.n

    P.wait("sync", s_st)
    P.wait("sync", s_mm)
    P.wait("gpsimd", s_mm)
    for c in range(8):
        P.dma("sync", abuf[:, c * 4096:(c + 1) * 4096],
              io["o_d"][:, c * 128:(c + 1) * 128], transpose=True, sem=s_ldo)
    for i in range(8):
        P.dma("gpsimd", abuf[:, 32768 + i * 1024:32768 + (i + 1) * 1024].rearrange(
                  "p (c x) -> p c x", c=8),
              io["woutr"][i].rearrange("c p x -> p c x"), sem=s_ld)
    otv = s_ldo.n
    wov = s_ld.n
    xres = fbuf[:, 1536:1536 + 4096]
    a3ev = {}
    for tj in range(8):
        if tj > 0:
            P.wait("sync", s_v)
        for c in range(8):
            P.dma("sync", xres[:, c * 512:(c + 1) * 512],
                  io["xf"][c * 128:(c + 1) * 128, tj * 512:(tj + 1) * 512],
                  sem=s_lds)
        ldsv = s_lds.n
        for i in range(8):
            pv = psA[:, (i % 2) * 512:(i % 2) * 512 + 512]
            if tj == 0 and i == 0:
                P.wait("tensor", s_ldo, otv)
                P.wait("tensor", s_ld, wov)
            if tj * 8 + i >= 2:
                P.wait("tensor", s_act, a3ev[tj * 8 + i - 2])
            for c in range(8):
                lw = abuf[:, 32768 + i * 1024 + c * 128:32768 + i * 1024 + (c + 1) * 128]
                rhs = abuf[:, c * 4096 + tj * 512: c * 4096 + (tj + 1) * 512]
                P.op("tensor", lambda e, o=pv, l=lw, r=rhs, c=c: e.matmul(
                    o, l, r, start=(c == 0), stop=(c == 7)),
                    [(s_mm, 1)] if c == 7 else ())
            P.wait("vector", s_mm)
            u_c = R.u[:, i * 512:(i + 1) * 512]
            if i == 0:
                P.wait("vector", s_lds, ldsv)
            P.op("vector", lambda e, o=pv, u_c=u_c, i=i: e.tensor_scalar(
                u_c, o, bo[:, i:i + 1], None, ALU.add), [(s_act, 1)])
            a3ev[tj * 8 + i] = s_act.n
            P.op("vector", lambda e, u_c=u_c, i=i: e.tensor_add(
                u_c, u_c, xres[:, i * 512:(i + 1) * 512]), [(s_v, 1)])
        P.wait("tensor", s_v)
        P.wait("scalar", s_v)
        emit_ln_tile(P, R, tj, io["x1f"], io.get("x1t"))
        P.wait("scalar", R.s_v)
        P.wait("scalar", R.s_mm)
    return [s_st, R.s_st[0], R.s_st[1], R.s_stg[0], R.s_stg[1]]


def build_l0attn_standalone():
    nc = bass.Bass("TRN2", target_bir_lowering=False)
    P = Plan(nc)
    io = {}
    io["xf"] = nc.dram_tensor("xf", [D, T], F32, kind="ExternalInput").ap()
    io["wqkr"] = nc.dram_tensor("wqkr", [16, 8, 128, 128], F32, kind="ExternalInput").ap()
    io["wvr"] = nc.dram_tensor("wvr", [2, 8, 128, 512], F32, kind="ExternalInput").ap()
    io["woutr"] = nc.dram_tensor("woutr", [8, 8, 128, 128], F32, kind="ExternalInput").ap()
    io["b_qk"] = nc.dram_tensor("b_qk", [64, 32], F32, kind="ExternalInput").ap()
    io["b_v"] = nc.dram_tensor("b_v", [1, 1024], F32, kind="ExternalInput").ap()
    io["b_out"] = nc.dram_tensor("b_out", [128, 8], F32, kind="ExternalInput").ap()
    io["qk_d"] = nc.dram_tensor("qk_d", [16, 2, 64, T], BF16, kind="Internal").ap()
    io["v_d"] = nc.dram_tensor("v_d", [T, 1024], BF16, kind="Internal").ap()
    io["o_d"] = nc.dram_tensor("o_d", [T, 1024], BF16, kind="Internal").ap()
    io["x1f"] = nc.dram_tensor("x1f", [D, T], F32, kind="ExternalOutput").ap()
    io["x1t"] = nc.dram_tensor("x1t", [D, T], BF16, kind="ExternalOutput").ap()
    P.a2_pexp = P.sbuf("at_pexp", [128, 2048 + 512], BF16)
    fin = emit_l0attn(P, io)
    P.build(fin)
    return nc


def prep_moe(inp):
    f = lambda a: np.ascontiguousarray(np.asarray(a, dtype=np.float32))
    w1 = f(inp["L0_ffn_w1"])
    w2 = f(inp["L0_ffn_w2"])
    return {
        "gwt": f(inp["L0_ffn_gate_w"]).T.copy(),
        "w1r": np.ascontiguousarray(
            w1.reshape(8, 8, 128, 32, 128).transpose(0, 3, 1, 2, 4)),
        "w2r": np.ascontiguousarray(
            w2.reshape(8, 32, 128, 8, 128).transpose(0, 3, 1, 2, 4)),
        "b1": np.ascontiguousarray(
            f(inp["L0_ffn_b1"]).reshape(8, 32, 128).transpose(0, 2, 1)),
        "b2": np.ascontiguousarray(
            f(inp["L0_ffn_b2"]).reshape(8, 8, 128).transpose(0, 2, 1)),
    }


def emit_moe_gate(P, io):
    s_ld = P.sem("mg_ld")
    s_mm = P.sem("mg_mm")
    s_v = P.sem("mg_v")
    s_a = P.sem("mg_a")
    s_st = P.sem("mg_st")

    c_erow = P.const(np.tile(np.arange(8, dtype=np.float32), (128, 1)), "c_erow")
    xa = P.sbuf("mg_xa", [128, 8 * 4096], F32)
    gw = P.sbuf("mg_gw", [128, 64], F32)
    erow = P.sbuf("mg_er", [128, 8], F32)
    ident = P.sbuf("mg_id", [128, 128], F32)
    msb = P.sbuf("mg_m", [128, 8 * 32], F32)
    mrow = P.sbuf("mg_mr", [8, 4096], F32)
    zrow = P.sbuf("mg_zr", [8, 4096], F32)
    cums = P.sbuf("mg_cu", [8, 4096], F32)
    obuf = P.sbuf("mg_ob", [128, 3 * 32 + 2], F32)
    work = P.sbuf("mg_wk", [128, 24], F32)
    ps = P.psum("mg_ps", [128, 512], F32)
    ps2 = P.psum("mg_ps2", [128, 512], F32)

    P.op("gpsimd", lambda e: e.memset(ident[:, :], 0.0))
    P.op("gpsimd", lambda e: e.affine_select(
        out=ident[:, :], in_=ident[:, :], compare_op=ALU.not_equal, fill=1.0,
        base=0, pattern=[[-1, 128]], channel_multiplier=1), [(s_v, 1)])
    P.op("vector", lambda e: e.memset(zrow[:, :], 0.0), [(s_v, 1)])
    for c in range(8):
        P.dma("sync", xa[:, c * 4096:(c + 1) * 4096],
              io["x1f"][c * 128:(c + 1) * 128, :], sem=s_ld)
    P.dma("sync", gw[:, :].rearrange("p (c e) -> p c e", c=8),
          io["gwt"].rearrange("(c p) e -> p c e", p=128), sem=s_ld)
    P.dma("sync", erow[:, :], c_erow, sem=s_ld)
    ldv = s_ld.n
    P.wait("tensor", s_ld, ldv)
    P.wait("tensor", s_v)
    for i in range(32):
        lg = ps[:, (i % 2) * 16:(i % 2) * 16 + 8]
        if i >= 2:
            P.wait("tensor", s_a)
        for c in range(8):
            lw = xa[:, c * 4096 + i * 128:c * 4096 + (i + 1) * 128]
            P.op("tensor", lambda e, o=lg, l=lw, c=c: e.matmul(
                o, l, gw[:, c * 8:(c + 1) * 8], start=(c == 0), stop=(c == 7)),
                [(s_mm, 1)] if c == 7 else ())
        P.wait("vector", s_mm)
        mx = work[:, 0:1]
        P.op("vector", lambda e, lg=lg: e.reduce_max(mx, lg, axis=AX.X), [(s_v, 1)])
        P.wait("vector", s_v)
        nmx = work[:, 1:2]
        P.op("vector", lambda e: e.tensor_scalar(nmx, mx, -1.0, None, ALU.mult))
        mcols = msb[:, i * 8:(i + 1) * 8]
        P.op("vector", lambda e, lg=lg, mcols=mcols: e.tensor_scalar(
            mcols, lg, mx, None, ALU.is_equal), [(s_v, 1)])
        P.wait("scalar", s_v)
        sume = work[:, 2:3]
        P.op("scalar", lambda e, lg=lg: e.activation(
            work[:, 4:12], lg, AF.Exp, bias=nmx, accum_out=sume), [(s_a, 1)])
        P.wait("vector", s_a)
        P.op("vector", lambda e, i=i: e.reciprocal(obuf[:, i:i + 1], sume))
        P.op("vector", lambda e, mcols=mcols: e.tensor_mul(work[:, 12:20], mcols, erow[:, :]))
        P.op("vector", lambda e, i=i: e.reduce_sum(
            obuf[:, 64 + i:65 + i], work[:, 12:20], axis=AX.X), [(s_v, 1)])
        P.wait("tensor", s_v)
        P.op("tensor", lambda e, mcols=mcols: e.transpose(
            ps2[0:8, 0:128], mcols, ident[:, :]), [(s_mm, 1)])
        P.wait("vector", s_mm)
        P.op("vector", lambda e, i=i: e.tensor_copy(
            mrow[:, i * 128:(i + 1) * 128], ps2[0:8, 0:128]), [(s_v, 1)])
    P.op("vector", lambda e: e.tensor_tensor_scan(
        cums[:, :], mrow[:, :], zrow[:, :], 0.0, ALU.add, ALU.add), [(s_v, 1)])
    P.wait("tensor", s_v)
    P.op("tensor", lambda e: e.transpose(ps2[0:32, 0:8], cums[:, 4064:4096],
                                         ident[0:8, 0:8]), [(s_mm, 1)])
    P.wait("vector", s_mm)
    P.op("vector", lambda e: e.tensor_copy(work[0:32, 16:24], ps2[0:32, 0:8]), [(s_v, 1)])
    P.wait("sync", s_v)
    P.dma("sync", io["counts"], work[31:32, 16:24], sem=s_st)
    for i in range(32):
        P.wait("tensor", s_v)
        P.op("tensor", lambda e, i=i: e.transpose(
            ps2[0:128, 8:16], mrow[:, i * 128:(i + 1) * 128] if False else
            cums[:, i * 128:(i + 1) * 128], ident[0:8, 0:8]), [(s_mm, 1)])
        P.wait("vector", s_mm)
        mcols = msb[:, i * 8:(i + 1) * 8]
        P.op("vector", lambda e, mcols=mcols: e.tensor_sub(
            work[:, 4:12], ps2[0:128, 8:16], mcols))
        P.op("vector", lambda e, mcols=mcols: e.tensor_mul(
            work[:, 4:12], work[:, 4:12], mcols))
        P.op("vector", lambda e, i=i: e.reduce_sum(
            obuf[:, 32 + i:33 + i], work[:, 4:12], axis=AX.X), [(s_v, 1)])
    P.wait("sync", s_v)
    P.dma("sync", io["pout"], obuf[:, 0:32], sem=s_st)
    P.dma("sync", io["rout"], obuf[:, 32:64], sem=s_st)
    P.dma("sync", io["esout"], obuf[:, 64:96], sem=s_st)
    return [s_st]


def build_moegate_standalone():
    nc = bass.Bass("TRN2", target_bir_lowering=False)
    P = Plan(nc)
    io = {}
    io["x1f"] = nc.dram_tensor("x1f", [D, T], F32, kind="ExternalInput").ap()
    io["gwt"] = nc.dram_tensor("gwt", [D, 8], F32, kind="ExternalInput").ap()
    io["counts"] = nc.dram_tensor("counts", [1, 8], F32, kind="ExternalOutput").ap()
    io["pout"] = nc.dram_tensor("pout", [128, 32], F32, kind="ExternalOutput").ap()
    io["rout"] = nc.dram_tensor("rout", [128, 32], F32, kind="ExternalOutput").ap()
    io["esout"] = nc.dram_tensor("esout", [128, 32], F32, kind="ExternalOutput").ap()
    fin = emit_moe_gate(P, io)
    P.build(fin)
    return nc


def emit_moe_compute(P, io):
    nc = P.nc
    s_lds = P.sem("mc_lds")
    s_ldt = [P.sem("mc_ldt0"), P.sem("mc_ldt1")]
    s_ldx = P.sem("mc_ldx")
    s_mm = P.sem("mc_mm")
    s_a = P.sem("mc_a")
    s_v = P.sem("mc_v")
    s_sc = [P.sem("mc_sc0"), P.sem("mc_sc1")]
    s_g = [P.sem("mc_g0"), P.sem("mc_g1")]
    s_wld = [P.sem("mc_w0"), P.sem("mc_w1")]
    s_zst = P.sem("mc_zst")
    s_yst = P.sem("mc_yst")

    c_erow = P.const(np.tile(np.arange(8, dtype=np.float32), (128, 1)), "c_erow2")

    abuf = P.sbuf("mc_ab", [128, 32768 + 16384], BF16)
    HT, XH, YT = 0, 32768, 32768 + 8192
    wsl = P.sbuf("mc_wsl", [128, 8192], BF16)
    xtok = P.sbuf("mc_xt", [128, 2048], BF16)
    gt = P.sbuf("mc_gt", [128, 2048], BF16)
    ytok = P.sbuf("mc_yk", [128, 8192], BF16)
    zt = ytok
    fb = P.sbuf("mc_fb", [128, 96 + 16], F32)
    b1sb = P.sbuf("mc_b1", [128, 256], F32)
    b2sb = P.sbuf("mc_b2", [128, 64], F32)
    idxb = P.sbuf("mc_ix", [128, 32], I32)
    asb = P.sbuf("mc_as", [128, 4096], F32)
    xres = P.sbuf("mc_xr", [128, 4096], F32)
    cnts = P.sbuf("mc_cn", [8, 9], F32)
    cob = P.sbuf("mc_co", [128, 8], F32)
    erow = P.sbuf("mc_er", [128, 8], F32)
    ident = P.sbuf("mc_id", [128, 128], F32)
    identb = P.sbuf("mc_idb", [128, 128], BF16)
    psA = P.psum("mc_psA", [128, 1024], F32)
    psB = P.psum("mc_psB", [128, 2048], F32)
    psT = P.psum("mc_psT", [128, 256], BF16)
    psF = P.psum("mc_psF", [128, 128], F32)
    R = LNRes(P, "mcln", stat_ps=psB[0:1, 0:1024], bc_ps=psB[:, 1024:2048])

    P.op("gpsimd", lambda e: e.memset(ident[:, :], 0.0))
    P.op("gpsimd", lambda e: e.affine_select(
        out=ident[:, :], in_=ident[:, :], compare_op=ALU.not_equal, fill=1.0,
        base=0, pattern=[[-1, 128]], channel_multiplier=1))
    P.op("gpsimd", lambda e: e.memset(identb[:, :], 0.0))
    P.op("gpsimd", lambda e: e.affine_select(
        out=identb[:, :], in_=identb[:, :], compare_op=ALU.not_equal, fill=1.0,
        base=0, pattern=[[-1, 128]], channel_multiplier=1), [(s_v, 1)])
    P.dma("sync", cnts[:, 0:8], io["counts_all"], sem=s_lds)
    P.dma("sync", cnts[:, 8:9], io["msel"], sem=s_lds)
    P.dma("sync", erow[:, :], c_erow, sem=s_lds)
    P.dma("sync", fb[:, 0:32], io["p"], sem=s_lds)
    P.dma("sync", fb[:, 32:64], io["r"], sem=s_lds)
    P.dma("sync", fb[:, 64:96], io["es"], sem=s_lds)
    P.dma("sync", b1sb[:, :].rearrange("p (e o) -> p e o", e=8),
          io["b1"].rearrange("e p o -> p e o"), sem=s_lds)
    P.dma("sync", b2sb[:, :].rearrange("p (e o) -> p e o", e=8),
          io["b2"].rearrange("e p o -> p e o"), sem=s_lds)
    P.wait("tensor", s_lds)
    P.wait("tensor", s_v)
    P.op("tensor", lambda e: e.matmul(psA[0:1, 0:8], cnts[:, 8:9], cnts[:, 0:8],
                                      start=True, stop=True), [(s_mm, 1)])
    P.wait("vector", s_mm)
    P.op("vector", lambda e: e.tensor_copy(cnts[0:1, 0:8], psA[0:1, 0:8]), [(s_v, 1)])
    P.wait("tensor", s_v)
    P.op("tensor", lambda e: e.matmul(psA[:, 512:520], R.ones_row[0:1, :],
                                      cnts[0:1, 0:8], start=True, stop=True),
         [(s_mm, 1)])
    P.wait("vector", s_mm)
    P.op("vector", lambda e: e.tensor_copy(cob[:, :], psA[:, 512:520]), [(s_v, 1)])

    P.op("vector", lambda e: e.memset(zt[:, :], 0.0), [(s_v, 1)])
    P.wait("sync", s_v)
    for b in range(8):
        P.dma("sync", io["bins_in"].rearrange("(a p) x -> p a x", p=128)
              [:, b * 8:(b + 1) * 8, :], zt[:, :].rearrange("p (a x) -> p a x", a=8),
              sem=s_zst)
    P.dma("sync", io["bins_in"][8192:8320, :], zt[:, 0:1024], sem=s_zst)
    P.dma("sync", io["bins_out"][8192:8320, :], zt[:, 0:1024], sem=s_zst)
    zv = s_zst.n

    wk = fb[:, 96:112]
    for i in range(32):
        sl = i % 2
        xts = xtok[:, sl * 1024:(sl + 1) * 1024]
        P.wait("sync", s_sc[sl])
        if i == 0:
            P.wait("sync", s_zst, zv)
        P.dma("sync", xts, io["x1t"][:, i * 128:(i + 1) * 128], transpose=True,
              sem=s_ldt[sl])
        es_c = fb[:, 64 + i:65 + i]
        r_c = fb[:, 32 + i:33 + i]
        P.op("vector", lambda e, es_c=es_c: e.tensor_scalar(
            wk[:, 0:8], erow[:, :], es_c, None, ALU.is_equal))
        P.op("vector", lambda e: e.tensor_mul(wk[:, 0:8], wk[:, 0:8], cob[:, :]))
        P.op("vector", lambda e: e.reduce_sum(wk[:, 8:9], wk[:, 0:8], axis=AX.X))
        P.op("vector", lambda e, r_c=r_c: e.tensor_add(wk[:, 9:10], wk[:, 8:9], r_c))
        P.op("vector", lambda e: e.tensor_single_scalar(
            wk[:, 10:11], wk[:, 9:10], 5120.0, ALU.is_lt))
        P.op("vector", lambda e, i=i: e.tensor_mul(
            fb[:, 0 + i:1 + i], fb[:, 0 + i:1 + i], wk[:, 10:11]))
        P.op("vector", lambda e, es_c=es_c, r_c=r_c: e.scalar_tensor_tensor(
            wk[:, 12:13], es_c, 1024.0, r_c, ALU.mult, ALU.add))
        P.op("vector", lambda e, r_c=r_c: e.tensor_scalar(
            wk[:, 13:14], wk[:, 12:13], 8192.0,
            None, ALU.min))
        P.op("vector", lambda e, r_c=r_c: e.tensor_single_scalar(
            wk[:, 11:12], r_c, 1024.0, ALU.is_ge))
        P.op("vector", lambda e: e.scalar_tensor_tensor(
            wk[:, 13:14], wk[:, 11:12], 8192.0, wk[:, 13:14], ALU.mult, ALU.max))
        P.op("vector", lambda e, i=i: e.tensor_copy(idxb[:, i:i + 1], wk[:, 13:14]),
             [(s_v, 1)])
        P.wait("gpsimd", s_v)
        P.wait("gpsimd", s_ldt[sl])
        P.op("gpsimd", lambda e, xts=xts, i=i: e.indirect_dma_start(
            io["bins_in"], IndirectOffsetOnAxis(idxb[:, i:i + 1], 0),
            xts, None), [(s_sc[sl], 16)], _dma=True)

    P.wait("sync", s_sc[0])
    P.wait("sync", s_sc[1])
    slot_use = {0: 0, 1: 0}
    xh_use = 0
    yst_last = 0
    for e_i in range(8):
        P.wait("sync", s_mm, xh_use)
        for c in range(8):
            P.dma("sync", abuf[:, XH + c * 1024:XH + (c + 1) * 1024],
                  io["bins_in"][e_i * 1024:(e_i + 1) * 1024, c * 128:(c + 1) * 128],
                  transpose=True, sem=s_ldx)
        xhv = s_ldx.n
        for o in range(32):
            sl = o % 2
            P.wait("gpsimd", s_mm, slot_use[sl])
            P.dma("gpsimd", wsl[:, sl * 4096:sl * 4096 + 1024].rearrange(
                      "p (c x) -> p c x", c=8),
                  io["w1r"][e_i, o].rearrange("c p x -> p c x"), sem=s_wld[sl])
            for hh in range(2):
                pv = psA[:, hh * 512:(hh + 1) * 512]
                if o == 0 and hh == 0:
                    P.wait("tensor", s_ldx, xhv)
                    P.wait("tensor", s_wld[sl])
                elif hh == 0:
                    P.wait("tensor", s_wld[sl])
                P.wait("tensor", s_a)
                for c in range(8):
                    lw = wsl[:, sl * 4096 + c * 128:sl * 4096 + (c + 1) * 128]
                    rhs = abuf[:, XH + c * 1024 + hh * 512:XH + c * 1024 + (hh + 1) * 512]
                    P.op("tensor", lambda e, o2=pv, l=lw, r=rhs, c=c: e.matmul(
                        o2, l, r, start=(c == 0), stop=(c == 7)),
                        [(s_mm, 1)] if c == 7 else ())
                P.wait("scalar", s_mm)
                P.op("scalar", lambda e, pv=pv, e_i=e_i, o=o, hh=hh: e.activation(
                    abuf[:, HT + o * 1024 + hh * 512:HT + o * 1024 + (hh + 1) * 512],
                    pv, AF.Relu, bias=b1sb[:, e_i * 32 + o:e_i * 32 + o + 1]),
                    [(s_a, 1)])
            slot_use[sl] = s_mm.n
        xh_use = s_mm.n
        for o2 in range(8):
            sl = o2 % 2
            P.wait("gpsimd", s_mm, slot_use[sl])
            P.wait("gpsimd", s_a)
            for q in range(4):
                P.dma("gpsimd", wsl[:, sl * 4096 + q * 1024:sl * 4096 + (q + 1) * 1024]
                      .rearrange("p (c x) -> p c x", c=8),
                      io["w2r"][e_i, o2, q * 8:(q + 1) * 8].rearrange("c p x -> p c x"),
                      sem=s_wld[sl])
            for hh in range(2):
                pv = psA[:, hh * 512:(hh + 1) * 512]
                if hh == 0:
                    P.wait("tensor", s_wld[sl])
                    P.wait("tensor", s_a)
                P.wait("tensor", s_v)
                for c in range(32):
                    lw = wsl[:, sl * 4096 + c * 128:sl * 4096 + (c + 1) * 128]
                    rhs = abuf[:, HT + c * 1024 + hh * 512:HT + c * 1024 + (hh + 1) * 512]
                    P.op("tensor", lambda e, o3=pv, l=lw, r=rhs, c=c: e.matmul(
                        o3, l, r, start=(c == 0), stop=(c == 31)),
                        [(s_mm, 1)] if c == 31 else ())
                P.wait("vector", s_mm)
                P.op("vector", lambda e, pv=pv, e_i=e_i, o2=o2, hh=hh: e.tensor_scalar(
                    abuf[:, YT + o2 * 1024 + hh * 512:YT + o2 * 1024 + (hh + 1) * 512],
                    pv, b2sb[:, e_i * 8 + o2:e_i * 8 + o2 + 1], None, ALU.add),
                    [(s_v, 1)])
            slot_use[sl] = s_mm.n
        P.wait("tensor", s_v)
        P.wait("vector", s_yst, yst_last)
        for rb in range(8):
            for c in range(8):
                P.op("tensor", lambda e, rb=rb, c=c: e.transpose(
                    psT[:, 0:128], abuf[:, YT + c * 1024 + rb * 128:
                                        YT + c * 1024 + (rb + 1) * 128],
                    identb[:, :]), [(s_mm, 1)])
                P.wait("vector", s_mm)
                P.op("vector", lambda e, rb=rb, c=c: e.tensor_copy(
                    ytok[:, rb * 1024 + c * 128:rb * 1024 + (c + 1) * 128],
                    psT[:, 0:128]), [(s_v, 1)])
                P.wait("tensor", s_v)
        P.wait("sync", s_v)
        for rb in range(8):
            P.dma("sync", io["bins_out"][e_i * 1024 + rb * 128:
                                         e_i * 1024 + (rb + 1) * 128, :],
                  ytok[:, rb * 1024:(rb + 1) * 1024], sem=s_yst)
        yst_last = s_yst.n

    P.wait("sync", s_yst)
    P.wait("gpsimd", s_yst)
    g_use = {0: 0, 1: 0}
    for tj in range(8):
        if tj > 0:
            P.wait("sync", s_v)
        for c in range(8):
            P.dma("sync", xres[:, c * 512:(c + 1) * 512],
                  io["x1f"][c * 128:(c + 1) * 128, tj * 512:(tj + 1) * 512],
                  sem=s_lds)
        ldsv = s_lds.n
        for sub in range(4):
            i = tj * 4 + sub
            sl = i % 2
            gts = gt[:, sl * 1024:(sl + 1) * 1024]
            P.wait("vector", s_g[sl])
            P.op("vector", lambda e, gts=gts: e.memset(gts, 0.0), [(s_v, 1)])
            P.wait("gpsimd", s_v)
            P.op("gpsimd", lambda e, gts=gts, i=i: e.indirect_dma_start(
                gts, None, io["bins_out"], IndirectOffsetOnAxis(idxb[:, i:i + 1], 0)),
                [(s_g[sl], 16)], _dma=True)
            P.wait("vector", s_g[sl])
            P.op("vector", lambda e, gts=gts, i=i, sub=sub: e.tensor_scalar(
                asb[:, sub * 1024:(sub + 1) * 1024], gts, fb[:, i:i + 1], None,
                ALU.mult), [(s_v, 1)])
        P.wait("tensor", s_v)
        P.wait("tensor", s_lds, ldsv)
        for c in range(8):
            for sub in range(4):
                P.op("tensor", lambda e, c=c, sub=sub: e.transpose(
                    psF[:, :], asb[:, sub * 1024 + c * 128:sub * 1024 + (c + 1) * 128],
                    ident[:, :]), [(s_mm, 1)])
                P.wait("vector", s_mm)
                u_c = R.u[:, c * 512:(c + 1) * 512]
                P.op("vector", lambda e, u_c=u_c, c=c, sub=sub: e.tensor_add(
                    u_c[:, sub * 128:(sub + 1) * 128], psF[:, :],
                    xres[:, c * 512 + sub * 128:c * 512 + (sub + 1) * 128]),
                    [(s_v, 1)])
                P.wait("tensor", s_v)
        P.wait("tensor", s_v)
        P.wait("scalar", s_v)
        emit_ln_tile(P, R, tj, io["x2f"], io.get("x2t"))
        P.wait("scalar", R.s_v)
        P.wait("scalar", R.s_mm)
        P.wait("vector", R.s_mm)
    return [R.s_st[0], R.s_st[1], R.s_stg[0], R.s_stg[1]]


def build_moec_standalone():
    nc = bass.Bass("TRN2", target_bir_lowering=False)
    P = Plan(nc)
    io = {}
    io["x1f"] = nc.dram_tensor("x1f", [D, T], F32, kind="ExternalInput").ap()
    io["x1t"] = nc.dram_tensor("x1t", [D, T], BF16, kind="ExternalInput").ap()
    io["counts_all"] = nc.dram_tensor("counts_all", [8, 8], F32, kind="ExternalInput").ap()
    io["msel"] = nc.dram_tensor("msel", [8, 1], F32, kind="ExternalInput").ap()
    io["p"] = nc.dram_tensor("p", [128, 32], F32, kind="ExternalInput").ap()
    io["r"] = nc.dram_tensor("r", [128, 32], F32, kind="ExternalInput").ap()
    io["es"] = nc.dram_tensor("es", [128, 32], F32, kind="ExternalInput").ap()
    io["w1r"] = nc.dram_tensor("w1r", [8, 32, 8, 128, 128], F32, kind="ExternalInput").ap()
    io["w2r"] = nc.dram_tensor("w2r", [8, 8, 32, 128, 128], F32, kind="ExternalInput").ap()
    io["b1"] = nc.dram_tensor("b1", [8, 128, 32], F32, kind="ExternalInput").ap()
    io["b2"] = nc.dram_tensor("b2", [8, 128, 8], F32, kind="ExternalInput").ap()
    io["bins_in"] = nc.dram_tensor("bins_in", [8320, 1024], BF16, kind="Internal").ap()
    io["bins_out"] = nc.dram_tensor("bins_out", [8320, 1024], BF16, kind="Internal").ap()
    io["x2f"] = nc.dram_tensor("x2f", [D, T], F32, kind="ExternalOutput").ap()
    io["x2t"] = nc.dram_tensor("x2t", [D, T], BF16, kind="ExternalOutput").ap()
    fin = emit_moe_compute(P, io)
    P.build(fin)
    return nc


def prep_gla(inp):
    f = lambda a: np.ascontiguousarray(np.asarray(a, dtype=np.float32))
    up_w = f(inp["L1_attn_up_w"])
    gate_w = f(inp["L1_attn_gate_w"])
    down_w = f(inp["L1_attn_down_w"])
    qk = np.concatenate([f(inp["L1_attn_q_w"]).T, f(inp["L1_attn_k_w"]).T], 1)
    gt = np.concatenate([f(inp["L1_attn_g_w"]).T, f(inp["L1_attn_tau_w"]).T], 1)
    return {
        "up_r": _wcols(up_w.T),
        "gate_r": np.ascontiguousarray(
            gate_w.T.reshape(8, 128, 4, 512).transpose(2, 0, 1, 3)),
        "down_r": _wcols(down_w.T),
        "qk_r": np.ascontiguousarray(qk.reshape(16, 128, 128)),
        "gt_r": np.ascontiguousarray(gt.reshape(16, 128, 128)),
        "ig1_r": np.ascontiguousarray(f(inp["L1_attn_ig_w1"]).T.reshape(16, 128, 16)),
        "ig2t": f(inp["L1_attn_ig_w2"]).T.copy(),
        "igb2": _pp(f(inp["L1_attn_ig_b2"])),
        "conv_r": f(inp["L1_attn_conv_w"]).reshape(16, 128, 4).copy(),
        "convb": _pp(f(inp["L1_attn_conv_b"])),
        "gb": f(inp["L1_attn_g_b"]).reshape(64, 1).copy(),
        "taub": f(inp["L1_attn_tau_b"]).reshape(64, 1).copy(),
        "rw": f(inp["L1_attn_residual_weight"]).reshape(1, 2048).copy(),
        "nw": f(inp["L1_attn_norm_weight"]).reshape(1, 2048).copy(),
    }


def emit_gla_full(P, io):
    nc = P.nc
    SCALE = 0.125
    s_lds = P.sem("gl_lds")
    s_ldw = P.sem("gl_ldw")
    s_ldr = P.sem("gl_ldr")
    s_ldg = [P.sem("gl_ldg0"), P.sem("gl_ldg1")]
    s_mm = P.sem("gl_mm")
    s_a = P.sem("gl_a")
    s_v = P.sem("gl_v")
    s_g = P.sem("gl_g")
    s_ost = [P.sem("gl_ost0"), P.sem("gl_ost1")]
    s_ldg = [P.sem("gl_g30"), P.sem("gl_g31")]
    s_ldd = [P.sem("gl_d0"), P.sem("gl_d1")]
    s_ldo = [P.sem("gl_ldo0"), P.sem("gl_ldo1")]
    s_atd = P.sem("gl_atd")

    c_indj = P.const((np.arange(64)[:, None] % 8 ==
                      np.arange(8)[None, :]).astype(np.float32), "c_indj")
    c_indi = P.const((np.arange(64)[:, None] // 8 ==
                      np.arange(8)[None, :]).astype(np.float32), "c_indi")
    c_tril8 = P.const(np.where(np.arange(8)[:, None] <= np.arange(8)[None, :],
                               1.0, 0.0).astype(np.float32), "c_tril8")

    A = P.sbuf("gl_A", [128, 45696], BF16)
    X2, XUP, SH, VT, SHT, ATB, SB_, T1B, QE, KD = (
        0, 4096, 12336, 20528, 28720, 36912, 37424, 39472, 39984, 40496)
    WB = P.sbuf("gl_WB", [128, 20736], BF16)
    ig2sb = P.sbuf("gl_ig2", [16, 2048], BF16)
    S = P.sbuf("gl_S", [128, 2048], F32)
    gm = P.sbuf("gl_gm", [64, 1024], F32)
    fsm = P.sbuf("gl_fsm", [128, 640], F32)
    cwb = P.sbuf("gl_cw", [128, 96], F32)
    gbt = P.sbuf("gl_gbt", [64, 2], F32)
    rows = P.sbuf("gl_rows", [1, 512], F32)
    bcast = P.sbuf("gl_bc", [128, 2048], BF16)
    ident = P.sbuf("gl_id", [128, 128], F32)
    identb = P.sbuf("gl_idb", [128, 128], BF16)
    indj = P.sbuf("gl_ij", [64, 8], F32)
    indi = P.sbuf("gl_ii", [64, 8], F32)
    tril8 = P.sbuf("gl_t8", [8, 8], F32)
    atd = P.sbuf("gl_atd", [8, 8], BF16)
    xres = P.sbuf("gl_xr", [128, 4096], F32)
    bc512 = P.sbuf("gl_b512", [128, 512], F32)
    gbb = P.sbuf("gl_gbb", [128, 128], F32)
    epsc = P.sbuf("gl_eps", [128, 2], F32)
    psG = P.psum("gl_psG", [128, 2048], F32)
    psT = P.psum("gl_psT", [128, 256], BF16)
    psF = P.psum("gl_psF", [128, 512], F32)
    R = LNRes(P, "glln", stat_ps=psG[0:1, 1024:2048], bc_ps=psF[:, 0:512],
              bc_sb=bc512)

    P.op("gpsimd", lambda e: e.memset(ident[:, :], 0.0))
    P.op("gpsimd", lambda e: e.affine_select(
        out=ident[:, :], in_=ident[:, :], compare_op=ALU.not_equal, fill=1.0,
        base=0, pattern=[[-1, 128]], channel_multiplier=1))
    P.op("gpsimd", lambda e: e.memset(identb[:, :], 0.0))
    P.op("gpsimd", lambda e: e.affine_select(
        out=identb[:, :], in_=identb[:, :], compare_op=ALU.not_equal, fill=1.0,
        base=0, pattern=[[-1, 128]], channel_multiplier=1), [(s_g, 1)])
    for o in range(16):
        P.dma("gpsimd", WB[:, o * 1024:(o + 1) * 1024].rearrange(
            "p (c x) -> p c x", c=8), io["up_r"][o].rearrange("c p x -> p c x"),
            sem=s_ldw)
    P.dma("gpsimd", WB[:, 16384:18432].rearrange("p (c x) -> p c x", c=16),
          io["qk_r"].rearrange("c p x -> p c x"), sem=s_ldw)
    P.dma("gpsimd", WB[:, 18432:20480].rearrange("p (c x) -> p c x", c=16),
          io["gt_r"].rearrange("c p x -> p c x"), sem=s_ldw)
    P.dma("gpsimd", WB[:, 20480:20736].rearrange("p (c x) -> p c x", c=16),
          io["ig1_r"].rearrange("c p x -> p c x"), sem=s_ldw)
    P.dma("gpsimd", ig2sb[:, :], io["ig2t"], sem=s_ldw)
    P.dma("sync", cwb[:, 0:64].rearrange("p (c x) -> p c x", c=16),
          io["conv_r"].rearrange("c p x -> p c x"), sem=s_lds)
    P.dma("sync", cwb[:, 64:80], io["convb"], sem=s_lds)
    P.dma("sync", cwb[:, 80:96], io["igb2"], sem=s_lds)
    P.dma("sync", gbt[:, 0:1], io["gb"], sem=s_lds)
    P.dma("sync", gbt[:, 1:2], io["taub"], sem=s_lds)
    P.dma("sync", indj[:, :], c_indj, sem=s_lds)
    P.dma("sync", indi[:, :], c_indi, sem=s_lds)
    P.dma("sync", tril8[:, :], c_tril8, sem=s_lds)
    pre = s_lds.n
    P.wait("tensor", s_lds, pre)
    P.wait("tensor", s_ldw)
    for h in range(4):
        P.dma("sync", rows[0:1, 0:512], io["rw"][0:1, h * 512:(h + 1) * 512],
              sem=s_ldr)
        P.wait("tensor", s_ldr)
        P.op("tensor", lambda e, h=h: e.matmul(
            psG[:, h * 512:(h + 1) * 512], R.ones_row[:, :],
            rows[0:1, 0:512], start=True, stop=True), [(s_mm, 1)])
        P.wait("vector", s_mm)
        P.op("vector", lambda e, h=h: e.tensor_copy(
            bcast[:, h * 512:(h + 1) * 512], psG[:, h * 512:(h + 1) * 512]),
            [(s_v, 1)])
        P.wait("sync", s_v)
    P.op("vector", lambda e: e.memset(epsc[:, 0:1], 1e-20))
    P.op("vector", lambda e: e.memset(epsc[:, 1:2], 1e-5))
    P.op("vector", lambda e: e.memset(S[0:64, :], 0.0))
    P.op("vector", lambda e: e.memset(gm[:, :], 0.0))
    P.op("vector", lambda e: e.memset(A[:, SB_:SB_ + 2048][0:64, :], 0.0))
    for o in range(16):
        P.op("vector", lambda e, o=o: e.memset(A[:, XUP + o * 515:XUP + o * 515 + 3],
                                               0.0))
    P.op("vector", lambda e: e.memset(A[:, QE:QE + 512], 0.0), [(s_v, 1)])

    g1_mm = {}
    sc_done = {}
    for tj in range(8):
        if tj > 0:
            P.wait("sync", s_mm, g1_mm[tj - 1])
        for c in range(8):
            P.dma("sync", A[:, X2 + c * 512:X2 + (c + 1) * 512],
                  io["x2t"][c * 128:(c + 1) * 128, tj * 512:(tj + 1) * 512],
                  sem=s_lds)
        ldv = s_lds.n
        for o in range(16):
            pv = psG[:, (o % 2) * 512:(o % 2) * 512 + 512]
            if o == 0:
                P.wait("tensor", s_lds, ldv)
                P.wait("tensor", s_v)
            if o >= 2 or tj > 0:
                P.wait("tensor", s_a)
            for c in range(8):
                lw = WB[:, o * 1024 + c * 128:o * 1024 + (c + 1) * 128]
                P.op("tensor", lambda e, o2=pv, l=lw, c=c: e.matmul(
                    o2, l, A[:, X2 + c * 512:X2 + (c + 1) * 512],
                    start=(c == 0), stop=(c == 7)), [(s_mm, 1)] if c == 7 else ())
            P.wait("scalar", s_mm)
            P.op("scalar", lambda e, pv=pv, o=o: e.activation(
                A[:, XUP + o * 515 + 3:XUP + (o + 1) * 515], pv, AF.Copy),
                [(s_a, 1)])
        P.wait("vector", s_a)
        if tj > 0:
            P.wait("vector", s_mm, g1_mm[tj - 1])
        for o in range(16):
            xo = XUP + o * 515
            so = SH + o * 512
            P.op("vector", lambda e, xo=xo, so=so, o=o: e.tensor_scalar(
                A[:, so:so + 512], A[:, xo:xo + 512],
                cwb[:, o * 4:o * 4 + 1], None, ALU.mult))
            for w in range(1, 4):
                P.op("vector", lambda e, xo=xo, so=so, o=o, w=w: e.scalar_tensor_tensor(
                    A[:, so:so + 512], A[:, xo + w:xo + w + 512],
                    cwb[:, o * 4 + w:o * 4 + w + 1], A[:, so:so + 512],
                    ALU.mult, ALU.add))
            P.op("vector", lambda e, xo=xo, o=o: e.tensor_copy(
                A[:, xo:xo + 3], A[:, xo + 512:xo + 515]), [(s_g, 1)])
        P.wait("scalar", s_g)
        for o in range(16):
            so = SH + o * 512
            P.op("scalar", lambda e, so=so, o=o: e.activation(
                A[:, so:so + 512], A[:, so:so + 512], AF.Silu,
                bias=cwb[:, 64 + o:65 + o]), [(s_a, 1)])
        P.wait("tensor", s_a, s_a.n - 16)
        pv1 = psG[0:16, 1024:1536]
        for c in range(16):
            lw = WB[:, 20480 + c * 16:20480 + (c + 1) * 16]
            P.op("tensor", lambda e, l=lw, c=c: e.matmul(
                pv1, l, A[:, XUP + c * 515 + 3:XUP + (c + 1) * 515],
                start=(c == 0), stop=(c == 15)), [(s_mm, 1)] if c == 15 else ())
        P.wait("vector", s_mm)
        P.op("vector", lambda e: e.tensor_copy(A[:, T1B:T1B + 512][0:16, :], pv1),
             [(s_v, 1)])
        P.wait("tensor", s_v)
        for c in range(16):
            pv = psG[:, (c % 2) * 512:(c % 2) * 512 + 512]
            P.wait("tensor", s_a)
            P.op("tensor", lambda e, c=c: e.matmul(
                pv, ig2sb[:, c * 128:(c + 1) * 128], A[:, T1B:T1B + 512][0:16, :],
                start=True, stop=True), [(s_mm, 1)])
            P.wait("scalar", s_mm)
            P.wait("scalar", s_v)
            P.op("scalar", lambda e, pv=pv, c=c: e.activation(
                fsm[:, 0:512], pv, AF.Sigmoid, bias=cwb[:, 80 + c:81 + c]),
                [(s_a, 1)])
            P.wait("vector", s_a)
            P.op("vector", lambda e, c=c: e.tensor_mul(
                A[:, SH + 0:SH + 0] if False else
                A[:, XUP + c * 515 + 3:XUP + (c + 1) * 515],
                fsm[:, 0:512], A[:, XUP + c * 515 + 3:XUP + (c + 1) * 515]),
                [(s_v, 1)])
        P.wait("tensor", s_v)
        for c4 in range(4):
            for c in range(16):
                P.op("tensor", lambda e, c4=c4, c=c: e.transpose(
                    psT[:, 0:128],
                    A[:, XUP + c * 515 + 3 + c4 * 128:XUP + c * 515 + 3 + (c4 + 1) * 128],
                    identb[:, :]), [(s_mm, 1)])
                P.wait("vector", s_mm)
                P.op("vector", lambda e, c4=c4, c=c: e.tensor_copy(
                    A[:, VT + c4 * 2048 + c * 128:VT + c4 * 2048 + (c + 1) * 128],
                    psT[:, 0:128]), [(s_v, 1)])
                P.wait("tensor", s_v)
        for c4 in range(4):
            for c in range(16):
                P.op("tensor", lambda e, c4=c4, c=c: e.transpose(
                    psT[:, 0:128], A[:, SH + c * 512 + c4 * 128:SH + c * 512 + (c4 + 1) * 128],
                    identb[:, :]), [(s_mm, 1)])
                P.wait("vector", s_mm)
                P.op("vector", lambda e, c4=c4, c=c: e.tensor_copy(
                    A[:, SHT + c4 * 2048 + c * 128:SHT + c4 * 2048 + (c + 1) * 128],
                    psT[:, 0:128]), [(s_v, 1)])
                P.wait("tensor", s_v)
        for c4 in range(4):
            t0 = c4 * 128
            P.wait("tensor", s_a)
            for c in range(16):
                P.op("tensor", lambda e, c=c, t0=t0: e.matmul(
                    psG[:, 1024:1152], A[:, SH + c * 512 + t0:SH + c * 512 + t0 + 128],
                    WB[:, 16384 + c * 128:16384 + (c + 1) * 128],
                    start=(c == 0), stop=(c == 15)))
            for c in range(16):
                P.op("tensor", lambda e, c=c, t0=t0: e.matmul(
                    psG[:, 1536:1664], A[:, SH + c * 512 + t0:SH + c * 512 + t0 + 128],
                    WB[:, 18432 + c * 128:18432 + (c + 1) * 128],
                    start=(c == 0), stop=(c == 15)), [(s_mm, 1)] if c == 15 else ())
            P.wait("vector", s_mm)
            P.op("vector", lambda e: e.tensor_scalar(
                fsm[:, 0:64], psG[:, 1024:1088], SCALE, None, ALU.mult))
            P.op("vector", lambda e: e.tensor_copy(fsm[:, 64:128], psG[:, 1088:1152]))
            P.op("vector", lambda e: e.tensor_copy(fsm[:, 128:192], psG[:, 1536:1600]))
            P.op("vector", lambda e: e.tensor_copy(fsm[:, 192:256], psG[:, 1600:1664]),
                 [(s_v, 1)])
            if tj == 0 and c4 == 0:
                P.wait("tensor", s_v)
                P.op("tensor", lambda e: e.transpose(
                    psF[0:1, 0:64], gbt[:, 0:1], ident[0:64, 0:64]))
                P.op("tensor", lambda e: e.transpose(
                    psF[0:1, 64:128], gbt[:, 1:2], ident[0:64, 0:64]), [(s_mm, 1)])
                P.wait("vector", s_mm)
                P.op("vector", lambda e: e.tensor_copy(rows[0:1, 0:128], psF[0:1, 0:128]),
                     [(s_v, 1)])
                P.wait("tensor", s_v)
                P.op("tensor", lambda e: e.matmul(
                    psF[:, 128:256], R.ones_row[:, :], rows[0:1, 0:128],
                    start=True, stop=True), [(s_mm, 1)])
                P.wait("vector", s_mm)
                P.op("vector", lambda e: e.tensor_copy(gbb[:, 0:64], psF[:, 128:192]))
                P.op("vector", lambda e: e.tensor_copy(gbb[:, 64:128], psF[:, 192:256]),
                     [(s_v, 1)])
            P.op("vector", lambda e: e.tensor_add(fsm[:, 128:192], fsm[:, 128:192],
                                                  gbb[:, 0:64]))
            P.op("vector", lambda e: e.tensor_add(fsm[:, 192:256], fsm[:, 192:256],
                                                  gbb[:, 64:128]), [(s_v, 1)])
            P.wait("scalar", s_v)
            P.op("scalar", lambda e: e.activation(fsm[:, 256:320], fsm[:, 128:192],
                                                  AF.Abs), [(s_a, 1)])
            P.wait("scalar", s_a)
            P.op("scalar", lambda e: e.activation(fsm[:, 256:320], fsm[:, 256:320],
                                                  AF.Exp, scale=-1.0), [(s_a, 1)])
            P.op("scalar", lambda e: e.activation(fsm[:, 256:320], fsm[:, 256:320],
                                                  AF.Ln, bias=1.0), [(s_a, 1)])
            P.wait("vector", s_a)
            P.op("vector", lambda e: e.tensor_scalar(fsm[:, 320:384], fsm[:, 128:192],
                                                     0.0, None, ALU.max))
            P.op("vector", lambda e: e.tensor_add(fsm[:, 256:320], fsm[:, 256:320],
                                                  fsm[:, 320:384]), [(s_v, 1)])
            P.wait("scalar", s_v)
            P.op("scalar", lambda e: e.activation(fsm[:, 320:384], fsm[:, 192:256],
                                                  AF.Sigmoid), [(s_a, 1)])
            P.wait("vector", s_a)
            P.op("vector", lambda e: e.scalar_tensor_tensor(
                fsm[:, 384:448], fsm[:, 256:320], -1.0, fsm[:, 320:384],
                ALU.mult, ALU.mult), [(s_v, 1)])
            P.wait("scalar", s_v)
            P.op("scalar", lambda e: e.activation(fsm[:, 448:512], fsm[:, 256:320],
                                                  AF.Ln, bias=epsc[:, 0:1]),
                 [(s_a, 1)])
            P.wait("vector", s_a)
            P.op("vector", lambda e: e.tensor_mul(fsm[:, 448:512], fsm[:, 448:512],
                                                  fsm[:, 320:384]), [(s_v, 1)])
            P.wait("scalar", s_v)
            P.op("scalar", lambda e: e.activation(fsm[:, 448:512], fsm[:, 448:512],
                                                  AF.Exp), [(s_a, 1)])
            P.op("scalar", lambda e: e.activation(fsm[:, 512:576], fsm[:, 64:128],
                                                  AF.Square), [(s_a, 1)])
            P.wait("vector", s_a)
            P.op("vector", lambda e: e.reduce_sum(fsm[:, 632:633], fsm[:, 512:576],
                                                  axis=AX.X), [(s_v, 1)])
            P.wait("scalar", s_v)
            P.op("scalar", lambda e: e.activation(fsm[:, 633:634], fsm[:, 632:633],
                                                  AF.Sqrt), [(s_a, 1)])
            P.wait("vector", s_a)
            P.op("vector", lambda e: e.tensor_scalar(fsm[:, 633:634], fsm[:, 633:634],
                                                     1e-12, None, ALU.max))
            P.op("vector", lambda e: e.reciprocal(fsm[:, 634:635], fsm[:, 633:634]))
            P.op("vector", lambda e: e.tensor_scalar(
                fsm[:, 512:576], fsm[:, 64:128], fsm[:, 634:635], None, ALU.mult))
            P.op("vector", lambda e: e.tensor_mul(fsm[:, 512:576], fsm[:, 512:576],
                                                  fsm[:, 448:512]), [(s_v, 1)])
            P.wait("tensor", s_v)
            P.op("tensor", lambda e: e.transpose(psF[0:64, 0:128], fsm[:, 384:448],
                                                 ident[:, :]), [(s_mm, 1)])
            P.wait("vector", s_mm)
            P.op("vector", lambda e: e.tensor_copy(gm[:, 0:128], psF[0:64, 0:128]),
                 [(s_v, 1)])
            P.wait("tensor", s_v)
            P.op("tensor", lambda e: e.transpose(psF[0:64, 128:256], fsm[:, 0:64],
                                                 ident[:, :]), [(s_mm, 1)])
            P.wait("vector", s_mm)
            P.op("vector", lambda e: e.tensor_copy(gm[:, 128:256], psF[0:64, 128:256]),
                 [(s_v, 1)])
            P.wait("tensor", s_v)
            P.op("tensor", lambda e: e.transpose(psF[0:64, 256:384], fsm[:, 512:576],
                                                 ident[:, :]), [(s_mm, 1)])
            P.wait("vector", s_mm)
            P.op("vector", lambda e: e.tensor_copy(gm[:, 256:384], psF[0:64, 256:384]),
                 [(s_v, 1)])
            P.op("vector", lambda e: e.tensor_tensor_scan(
                gm[:, 384:512], gm[:, 0:128], gm[:, 896:1024], 0.0, ALU.add, ALU.add),
                [(s_v, 1)])
            P.wait("scalar", s_v)
            P.op("scalar", lambda e: e.activation(gm[:, 512:640], gm[:, 384:512],
                                                  AF.Exp), [(s_a, 1)])
            P.wait("vector", s_a)
            P.op("vector", lambda e, c4=c4: e.tensor_mul(
                A[:, QE + c4 * 128:QE + (c4 + 1) * 128][0:64, :],
                gm[:, 128:256], gm[:, 512:640]), [(s_v, 1)])
            P.op("vector", lambda e: e.tensor_scalar(
                gm[:, 640:768], gm[:, 384:512], gm[:, 511:512], None, ALU.subtract),
                [(s_v, 1)])
            P.wait("scalar", s_v)
            P.op("scalar", lambda e: e.activation(gm[:, 640:768], gm[:, 640:768],
                                                  AF.Exp, scale=-1.0), [(s_a, 1)])
            P.wait("vector", s_a)
            P.wait("tensor", s_a)
            P.op("tensor", lambda e: e.transpose(psF[:, 384:448],
                                                 gm[:, 640:768][0:64, :],
                                                 ident[0:64, 0:64]), [(s_mm, 1)])
            P.wait("vector", s_mm)
            P.op("vector", lambda e, c4=c4: e.tensor_mul(
                A[:, KD + c4 * 64:KD + (c4 + 1) * 64], psF[:, 384:448],
                fsm[:, 512:576]), [(s_v, 1)])
            P.wait("scalar", s_v)
            P.op("scalar", lambda e, c4=c4: e.activation(
                gm[:, 768 + c4:769 + c4], gm[:, 511:512], AF.Exp), [(s_a, 1)])
            P.wait("vector", s_a)
            P.op("vector", lambda e: e.memset(gm[:, 772:900], 0.0), [(s_v, 1)])
            atb = A[:, ATB + c4 * 128:ATB + (c4 + 1) * 128]
            P.op("vector", lambda e, atb=atb: e.memset(atb, 0.0), [(s_v, 1)])
            for a in range(16):
                ja = 8 * a
                if a > 0:
                    P.op("vector", lambda e, ja=ja: e.tensor_scalar(
                        fsm[0:64, 576:577], gm[:, 384 + ja:385 + ja],
                        gm[:, 384 + ja - 8:385 + ja - 8], None, ALU.subtract),
                        [(s_v, 1)])
                    P.wait("scalar", s_v)
                    P.op("scalar", lambda e: e.activation(
                        fsm[0:64, 577:578], fsm[0:64, 576:577], AF.Exp), [(s_a, 1)])
                    P.wait("vector", s_a)
                    P.op("vector", lambda e, ja=ja: e.tensor_scalar(
                        gm[:, 772:772 + ja], gm[:, 772:772 + ja],
                        fsm[0:64, 577:578], None, ALU.mult))
                    P.op("vector", lambda e, ja=ja: e.tensor_scalar(
                        fsm[0:64, 584:592], gm[:, 384 + ja - 8:384 + ja],
                        gm[:, 384 + ja:385 + ja], None, ALU.subtract), [(s_v, 1)])
                    P.wait("scalar", s_v)
                    P.op("scalar", lambda e: e.activation(
                        fsm[0:64, 584:592], fsm[0:64, 584:592], AF.Exp, scale=-1.0),
                        [(s_a, 1)])
                    P.wait("vector", s_a)
                    P.op("vector", lambda e, ja=ja: e.tensor_mul(
                        gm[:, 772 + ja - 8:772 + ja], gm[:, 256 + ja - 8:256 + ja],
                        fsm[0:64, 584:592]), [(s_v, 1)])
                    P.op("vector", lambda e, ja=ja: e.tensor_scalar(
                        fsm[0:64, 592:600], gm[:, 384 + ja:384 + ja + 8],
                        gm[:, 384 + ja:385 + ja], None, ALU.subtract), [(s_v, 1)])
                    P.wait("scalar", s_v)
                    P.op("scalar", lambda e: e.activation(
                        fsm[0:64, 592:600], fsm[0:64, 592:600], AF.Exp), [(s_a, 1)])
                    P.wait("vector", s_a)
                    P.op("vector", lambda e, ja=ja: e.tensor_mul(
                        A[:, KD + 512:KD + 520][0:64, :], gm[:, 128 + ja:128 + ja + 8],
                        fsm[0:64, 592:600]))
                    P.op("vector", lambda e, ja=ja: e.tensor_copy(
                        A[:, KD + 520:KD + 520 + ja][0:64, :], gm[:, 772:772 + ja]),
                        [(s_v, 1)])
                    P.wait("tensor", s_v)
                    P.op("tensor", lambda e, ja=ja: e.matmul(
                        psF[0:ja, 0:8], A[:, KD + 520:KD + 520 + ja][0:64, :],
                        A[:, KD + 512:KD + 520][0:64, :], start=True, stop=True),
                        [(s_mm, 1)])
                    P.wait("vector", s_mm)
                    P.op("vector", lambda e, atb=atb, ja=ja: e.tensor_copy(
                        atb[0:ja, ja:ja + 8], psF[0:ja, 0:8]), [(s_v, 1)])
                P.op("vector", lambda e, ja=ja: e.tensor_sub(
                    gm[:, 900:964],
                    gm[:, 384 + ja:384 + ja + 8].unsqueeze(2).broadcast_to([64, 8, 8]),
                    gm[:, 384 + ja:384 + ja + 8].unsqueeze(1).broadcast_to([64, 8, 8])))
                P.op("vector", lambda e: e.tensor_scalar(
                    gm[:, 900:964], gm[:, 900:964], 0.0, None, ALU.min), [(s_v, 1)])
                P.wait("scalar", s_v)
                P.op("scalar", lambda e: e.activation(gm[:, 900:964], gm[:, 900:964],
                                                      AF.Exp), [(s_a, 1)])
                P.wait("vector", s_a)
                P.op("vector", lambda e, ja=ja: e.tensor_mul(
                    gm[:, 900:964], gm[:, 900:964],
                    gm[:, 128 + ja:128 + ja + 8].unsqueeze(2).broadcast_to([64, 8, 8])))
                P.op("vector", lambda e, ja=ja: e.tensor_mul(
                    gm[:, 900:964], gm[:, 900:964],
                    gm[:, 256 + ja:256 + ja + 8].unsqueeze(1).broadcast_to([64, 8, 8])),
                    [(s_v, 1)])
                P.wait("tensor", s_v)
                P.op("tensor", lambda e: e.matmul(
                    psF[0:1, 8:72], R.ones_col[0:64, 0:1], gm[:, 900:964],
                    start=True, stop=True), [(s_mm, 1)])
                P.wait("vector", s_mm)
                P.op("vector", lambda e: e.tensor_copy(rows[0:1, 128:192], psF[0:1, 8:72]),
                     [(s_v, 1)])
                P.wait("tensor", s_v)
                P.op("tensor", lambda e: e.transpose(
                    psF[0:64, 72:73], rows[0:1, 128:192], ident[0:1, 0:1]), [(s_mm, 1)])
                P.wait("vector", s_mm)
                P.op("vector", lambda e: e.tensor_scalar(
                    fsm[0:64, 600:608], indi[:, :], psF[0:64, 72:73], None, ALU.mult),
                    [(s_v, 1)])
                P.wait("tensor", s_v)
                P.op("tensor", lambda e: e.matmul(
                    psF[0:8, 80:88], indj[:, :], fsm[0:64, 600:608],
                    start=True, stop=True), [(s_mm, 1)])
                P.wait("vector", s_mm)
                P.op("vector", lambda e: e.tensor_mul(
                    atd[:, :], psF[0:8, 80:88], tril8[:, :]), [(s_v, 1)])
                P.wait("sync", s_v)
                P.dma("sync", atb[ja:ja + 8, ja:ja + 8], atd[:, :], sem=s_atd)
                P.wait("vector", s_atd)
            P.wait("tensor", s_v)
            P.wait("tensor", s_a)
            P.wait("tensor", s_atd)
            qe_l = A[:, QE + c4 * 128:QE + (c4 + 1) * 128][0:64, :]
            for h in range(4):
                P.op("tensor", lambda e, h=h, qe_l=qe_l: e.matmul(
                    psG[:, h * 512:(h + 1) * 512], qe_l,
                    A[:, SB_ + h * 512:SB_ + (h + 1) * 512][0:64, :],
                    start=True, stop=False))
            for h in range(4):
                P.op("tensor", lambda e, h=h, c4=c4, atb=atb: e.matmul(
                    psG[:, h * 512:(h + 1) * 512], atb,
                    A[:, VT + c4 * 2048 + h * 512:VT + c4 * 2048 + (h + 1) * 512],
                    start=False, stop=True), [(s_mm, 1)])
            P.wait("vector", s_mm)
            sl = c4 % 2
            P.wait("vector", s_ost[sl])
            for h in range(4):
                P.op("vector", lambda e, h=h, c4=c4: e.tensor_mul(
                    A[:, T1B:T1B + 512],
                    A[:, SHT + c4 * 2048 + h * 512:SHT + c4 * 2048 + (h + 1) * 512],
                    bcast[:, h * 512:(h + 1) * 512]))
                P.op("vector", lambda e, h=h, sl=sl: e.tensor_add(
                    A[:, KD + 1024 + sl * 2048 + h * 512:KD + 1024 + sl * 2048 + (h + 1) * 512],
                    psG[:, h * 512:(h + 1) * 512], A[:, T1B:T1B + 512]), [(s_v, 1)])
            P.wait("sync", s_v)
            P.dma("sync", io["o_buf"][(tj * 4 + c4) * 128:(tj * 4 + c4 + 1) * 128, :],
                  A[:, KD + 1024 + sl * 2048:KD + 1024 + (sl + 1) * 2048], sem=s_ost[sl])
            P.wait("tensor", s_v)
            for h in range(4):
                P.op("tensor", lambda e, h=h, c4=c4: e.matmul(
                    psG[0:64, h * 512:(h + 1) * 512], A[:, KD + c4 * 64:KD + (c4 + 1) * 64],
                    A[:, VT + c4 * 2048 + h * 512:VT + c4 * 2048 + (h + 1) * 512],
                    start=True, stop=True), [(s_mm, 1)] if h == 3 else ())
            P.wait("vector", s_mm)
            P.op("vector", lambda e, c4=c4: e.tensor_scalar(
                S[0:64, :], S[0:64, :], gm[:, 768 + c4:769 + c4], None, ALU.mult),
                [(s_v, 1)])
            P.op("vector", lambda e: e.tensor_add(S[0:64, :], S[0:64, :],
                                                  psG[0:64, 0:2048]), [(s_v, 1)])
            P.wait("scalar", s_v)
            P.op("scalar", lambda e: e.activation(
                A[:, SB_:SB_ + 2048][0:64, :], S[0:64, :], AF.Copy), [(s_a, 1)])
            sc_done[(tj, c4)] = s_mm.n
        g1_mm[tj] = s_mm.n
    return {"A": A, "WB": WB, "psG": psG, "psT": psT, "psF": psF, "R": R,
            "s_lds": s_lds, "s_mm": s_mm, "s_a": s_a, "s_v": s_v,
            "s_ost": s_ost, "s_ldo": s_ldo, "s_ldg": s_ldg, "s_ldd": s_ldd,
            "xres": xres, "bcast": bcast, "fsm": fsm, "identb": identb, "S": S,
            "g1_mm": g1_mm, "g3_mm": {}, "rows": rows, "epsc": epsc, "s_ldw": s_ldw, "s_ldr": s_ldr}


def emit_gla_g3(P, io, E):
    nc = P.nc
    A, WB, psG, psT, psF, R = E["A"], E["WB"], E["psG"], E["psT"], E["psF"], E["R"]
    s_lds, s_mm, s_a, s_v = E["s_lds"], E["s_mm"], E["s_a"], E["s_v"]
    s_ost, s_ldo, s_ldd = E["s_ost"], E["s_ldo"], E["s_ldd"]
    xres, bcast = E["xres"], E["bcast"]
    fsm = E["fsm"]
    X2 = 0
    XGT = 4096
    OT = 12288
    SFG = 16384
    SQD = 20480
    sqd = E["S"]

    P.wait("sync", s_mm)
    P.wait("sync", s_ost[0])
    P.wait("sync", s_ost[1])
    P.wait("gpsimd", s_mm)
    P.wait("gpsimd", s_a)
    rows_l = E["rows"]
    P.wait("sync", s_v)
    for h in range(4):
        P.dma("sync", rows_l[0:1, 0:512], io["nw"][0:1, h * 512:(h + 1) * 512],
              sem=E["s_ldr"])
        P.wait("tensor", E["s_ldr"])
        P.op("tensor", lambda e, h=h: e.matmul(
            psG[:, h * 512:(h + 1) * 512], R.ones_row[:, :],
            rows_l[0:1, 0:512], start=True, stop=True), [(s_mm, 1)])
        P.wait("vector", s_mm)
        P.op("vector", lambda e, h=h: e.tensor_copy(
            E["bcast"][:, h * 512:(h + 1) * 512], psG[:, h * 512:(h + 1) * 512]),
            [(s_v, 1)])
        P.wait("sync", s_v)
    for db in range(4):
        P.dma("gpsimd", WB[:, db * 4096:(db + 1) * 4096].rearrange(
            "p (c x) -> p c x", c=8), io["gate_r"][db].rearrange("c p x -> p c x"),
            sem=E["s_ldg"][0])
    gwv = E["s_ldg"][0].n
    dsl_use = {0: 0, 1: 0}
    for tj in range(8):
        if tj > 0:
            P.wait("sync", s_mm, E["g3_mm"][tj - 1])
        for c in range(8):
            P.dma("sync", A[:, X2 + c * 512:X2 + (c + 1) * 512],
                  io["x2t"][c * 128:(c + 1) * 128, tj * 512:(tj + 1) * 512],
                  sem=s_lds)
        for c in range(8):
            P.dma("sync", xres[:, c * 512:(c + 1) * 512],
                  io["x2f"][c * 128:(c + 1) * 128, tj * 512:(tj + 1) * 512],
                  sem=s_lds)
        ldv = s_lds.n
        for sub in range(4):
            i = tj * 4 + sub
            osl = sub % 2
            P.wait("sync", s_v, E.get(("oread", i - 2), 0))
            P.dma("sync", A[:, OT + osl * 2048:OT + (osl + 1) * 2048],
                  io["o_buf"][i * 128:(i + 1) * 128, :], sem=s_ldo[osl])
            for db in range(4):
                pv = psG[:, (db % 2) * 512:(db % 2) * 512 + 512]
                if tj == 0 and sub == 0 and db == 0:
                    P.wait("tensor", E["s_ldg"][0], gwv)
                    P.wait("tensor", s_lds, ldv)
                elif sub == 0 and db == 0:
                    P.wait("tensor", s_lds, ldv)
                P.wait("tensor", s_a)
                for c in range(8):
                    lw = A[:, X2 + c * 512 + sub * 128:X2 + c * 512 + sub * 128 + 128]
                    P.op("tensor", lambda e, o=pv, l=lw, c=c, db=db: e.matmul(
                        o, l, WB[:, db * 4096 + c * 512:db * 4096 + (c + 1) * 512],
                        start=(c == 0), stop=(c == 7)), [(s_mm, 1)] if c == 7 else ())
                P.wait("scalar", s_mm)
                P.op("scalar", lambda e, pv=pv, db=db: e.activation(
                    A[:, SFG + db * 512:SFG + (db + 1) * 512], pv, AF.Silu),
                    [(s_a, 1)])
            P.wait("vector", s_a)
            P.wait("vector", s_ldo[osl])
            P.op("vector", lambda e, osl=osl: e.tensor_mul(
                A[:, SFG:SFG + 2048], A[:, SFG:SFG + 2048],
                A[:, OT + osl * 2048:OT + (osl + 1) * 2048]), [(s_v, 1)])
            E[("oread", i)] = s_v.n
            P.wait("scalar", s_v)
            P.op("scalar", lambda e: e.activation(
                sqd[:, :], A[:, SFG:SFG + 2048], AF.Square,
                accum_out=fsm[:, 608:609]), [(s_a, 1)])
            P.wait("vector", s_a)
            P.op("vector", lambda e: e.tensor_scalar(
                fsm[:, 609:610], fsm[:, 608:609], 1.0 / 2048.0, None, ALU.mult),
                [(s_v, 1)])
            P.wait("scalar", s_v)
            P.op("scalar", lambda e: e.activation(
                fsm[:, 610:611], fsm[:, 609:610], AF.Sqrt, bias=E["epsc"][:, 1:2]),
                [(s_a, 1)])
            P.wait("vector", s_a)
            P.op("vector", lambda e: e.reciprocal(fsm[:, 611:612], fsm[:, 610:611]))
            P.op("vector", lambda e: e.tensor_scalar(
                A[:, SFG:SFG + 2048], A[:, SFG:SFG + 2048], fsm[:, 611:612],
                None, ALU.mult))
            P.op("vector", lambda e: e.tensor_mul(
                A[:, SFG:SFG + 2048], A[:, SFG:SFG + 2048], bcast[:, 0:2048]),
                [(s_v, 1)])
            P.wait("tensor", s_v)
            for c in range(16):
                P.op("tensor", lambda e, c=c: e.transpose(
                    psT[:, 0:128], A[:, SFG + c * 128:SFG + (c + 1) * 128],
                    E["identb"][:, :]), [(s_mm, 1)])
                P.wait("vector", s_mm)
                P.op("vector", lambda e, c=c, sub=sub: e.tensor_copy(
                    A[:, XGT + c * 512 + sub * 128:XGT + c * 512 + sub * 128 + 128],
                    psT[:, 0:128]), [(s_v, 1)])
                P.wait("tensor", s_v)
        for o in range(8):
            sl = o % 2
            P.wait("gpsimd", s_mm, dsl_use[sl])
            P.dma("gpsimd", WB[:, 16384 + sl * 2048:16384 + (sl + 1) * 2048]
                  .rearrange("p (c x) -> p c x", c=16),
                  io["down_r"][o].rearrange("c p x -> p c x"), sem=s_ldd[sl])
            pv = psG[:, sl * 512:(sl + 1) * 512]
            P.wait("tensor", s_ldd[sl])
            P.wait("tensor", s_v)
            for c in range(16):
                lw = WB[:, 16384 + sl * 2048 + c * 128:16384 + sl * 2048 + (c + 1) * 128]
                P.op("tensor", lambda e, o2=pv, l=lw, c=c: e.matmul(
                    o2, l, A[:, XGT + c * 512:XGT + (c + 1) * 512],
                    start=(c == 0), stop=(c == 15)), [(s_mm, 1)] if c == 15 else ())
            dsl_use[sl] = s_mm.n
            P.wait("vector", s_mm)
            u_c = R.u[:, o * 512:(o + 1) * 512]
            P.op("vector", lambda e, pv=pv, u_c=u_c, o=o: e.tensor_add(
                u_c, pv, xres[:, o * 512:(o + 1) * 512]), [(s_v, 1)])
        P.wait("tensor", s_v)
        P.wait("scalar", s_v)
        emit_ln_tile(P, R, tj, io["x3f"], io.get("x3t"))
        P.wait("scalar", R.s_v)
        P.wait("scalar", R.s_mm)
        P.wait("vector", R.s_mm)
        E["g3_mm"][tj] = s_mm.n
    return [R.s_st[0], R.s_st[1], R.s_stg[0], R.s_stg[1]]


def build_gla_standalone():
    nc = bass.Bass("TRN2", target_bir_lowering=False)
    P = Plan(nc)
    io = {}
    io["x2f"] = nc.dram_tensor("x2f", [D, T], F32, kind="ExternalInput").ap()
    io["x2t"] = nc.dram_tensor("x2t", [D, T], BF16, kind="ExternalInput").ap()
    for k, shp in (("up_r", (16, 8, 128, 128)), ("gate_r", (4, 8, 128, 512)),
                   ("down_r", (8, 16, 128, 128)), ("qk_r", (16, 128, 128)),
                   ("gt_r", (16, 128, 128)), ("ig1_r", (16, 128, 16)),
                   ("ig2t", (16, 2048)), ("igb2", (128, 16)),
                   ("conv_r", (16, 128, 4)), ("convb", (128, 16)),
                   ("gb", (64, 1)), ("taub", (64, 1)),
                   ("rw", (1, 2048)), ("nw", (1, 2048))):
        io[k] = nc.dram_tensor(k, list(shp), F32, kind="ExternalInput").ap()
    io["o_buf"] = nc.dram_tensor("o_buf", [T, 2048], BF16, kind="Internal").ap()
    io["x3f"] = nc.dram_tensor("x3f", [D, T], F32, kind="ExternalOutput").ap()
    io["x3t"] = nc.dram_tensor("x3t", [D, T], BF16, kind="ExternalOutput").ap()
    E = emit_gla_full(P, io)
    fin = emit_gla_g3(P, io, E)
    P.build(fin)
    return nc


_PROGS = {}


def _prog(name, builder):
    if name not in _PROGS:
        _PROGS[name] = builder()
    return _PROGS[name]


def kernel(**inputs):
    import ml_dtypes
    x = np.asarray(inputs["L0_attn_x"], dtype=np.float32)
    pa = prep_l0attn(inputs)
    pm = prep_moe(inputs)
    pg = prep_gla(inputs)
    pf = prep_l1ffn(inputs)

    nc1 = _prog("attn", build_l0attn_standalone)
    ims = []
    for c in range(8):
        im = dict(pa)
        im["xf"] = np.ascontiguousarray(x[c].T)
        ims.append(im)
    r1 = run_bass_kernel_spmd(nc1, ims, core_ids=list(range(8)))
    x1f = [r1.results[c]["x1f"] for c in range(8)]
    x1t = [r1.results[c]["x1t"] for c in range(8)]

    nc2 = _prog("gate", build_moegate_standalone)
    ims = [{"x1f": x1f[c], "gwt": pm["gwt"]} for c in range(8)]
    r2 = run_bass_kernel_spmd(nc2, ims, core_ids=list(range(8)))
    counts_all = np.concatenate([r2.results[c]["counts"] for c in range(8)], axis=0)
    counts_all = np.ascontiguousarray(counts_all)

    nc3 = _prog("moec", build_moec_standalone)
    ims = []
    for c in range(8):
        msel = np.zeros((8, 1), np.float32)
        msel[0:c] = 1.0
        im = {"x1f": x1f[c], "x1t": x1t[c], "counts_all": counts_all,
              "msel": msel, "p": r2.results[c]["pout"], "r": r2.results[c]["rout"],
              "es": r2.results[c]["esout"], "w1r": pm["w1r"], "w2r": pm["w2r"],
              "b1": pm["b1"], "b2": pm["b2"]}
        ims.append(im)
    r3 = run_bass_kernel_spmd(nc3, ims, core_ids=list(range(8)))

    nc4 = _prog("gla", build_gla_standalone)
    ims = []
    for c in range(8):
        im = dict(pg)
        im["x2f"] = r3.results[c]["x2f"]
        im["x2t"] = r3.results[c]["x2t"]
        ims.append(im)
    r4 = run_bass_kernel_spmd(nc4, ims, core_ids=list(range(8)))

    nc5 = _prog("ffn", build_l1ffn_standalone)
    ims = []
    for c in range(8):
        im = dict(pf)
        im["x3f"] = r4.results[c]["x3f"]
        im["x3t"] = r4.results[c]["x3t"]
        ims.append(im)
    r5 = run_bass_kernel_spmd(nc5, ims, core_ids=list(range(8)))

    out = np.stack([np.ascontiguousarray(r5.results[c]["outf"].T) for c in range(8)])
    return out.astype(np.float32)
